# version 47
# baseline (speedup 1.0000x reference)
import numpy as np
import ml_dtypes
import concourse.bass as bass
import concourse.mybir as mybir
from concourse.bass_utils import run_bass_kernel_spmd

F32 = mybir.dt.float32
BF16 = mybir.dt.bfloat16
AF = mybir.ActivationFunctionType
ALU = mybir.AluOpType
AX = mybir.AxisListType

D = 1024
EPS = 1e-5

ENGS = ("sync", "gpsimd", "tensor", "vector", "scalar")


class _Slot:
    __slots__ = ("h", "count")
    def __init__(self, h):
        self.h = h
        self.count = 0


class _Node:
    __slots__ = ("nid", "eng", "fn", "deps", "slot", "target", "milestone")


class G:

    def __init__(self, nc):
        self.nc = nc
        self.nodes = []
        self.eng_sem = {e: nc.alloc_semaphore(name=f"es_{e}") for e in ENGS}
        self.last_w = {}
        self.readers = {}
        self.gdep = {e: None for e in ENGS}
        self._slots = {}
        self._slot_last = {}

    def slot(self, name):
        s = self._slots.get(name)
        if s is None:
            s = _Slot(self.nc.alloc_semaphore(name=f"sl_{name}"))
            self._slots[name] = s
        return s

    def node(self, eng, fn, reads=(), writes=(), deps=(), slot=None):
        nid = len(self.nodes)
        d = set(deps)
        for b in reads:
            w = self.last_w.get(b)
            if w is not None:
                d.add(w)
        for b in writes:
            w = self.last_w.get(b)
            if w is not None:
                d.add(w)
            d.update(self.readers.get(b, ()))
        if self.gdep[eng] is not None:
            d.add(self.gdep[eng])
        d.discard(nid)
        n = _Node()
        n.nid = nid
        n.eng = eng
        n.fn = fn
        n.slot = slot
        n.target = None
        if slot is not None:
            slot.count += 16
            n.target = slot.count
            self._slot_last[id(slot)] = nid
        n.deps = sorted(d)
        self.nodes.append(n)
        for b in reads:
            self.readers.setdefault(b, set()).add(nid)
        for b in writes:
            self.last_w[b] = nid
            self.readers[b] = set()
        return nid

    def dma(self, eng, out_ap, in_ap, reads=(), writes=(), slot=None, deps=()):
        def fn(e, o=out_ap, i=in_ap):
            return e.dma_start(out=o, in_=i)
        return self.node(eng, fn, reads, writes, deps, slot=slot)

    def barrier(self):
        tails = set()
        for w in self.last_w.values():
            tails.add(w)
        for rs in self.readers.values():
            tails.update(rs)
        tails.update(self._slot_last.values())
        ids = {}
        for e in ENGS:
            ids[e] = self.node(e, lambda en: en.nop(), deps=tuple(tails))
        for e in ENGS:
            self.gdep[e] = ids[e]

    def emit(self):
        nodes = self.nodes
        needed = [False] * len(nodes)
        for n in nodes:
            for dd in n.deps:
                needed[dd] = True
        cnt = {e: 0 for e in ENGS}
        for n in nodes:
            if n.slot is not None:
                n.milestone = (n.slot.h, n.target)
            elif needed[n.nid]:
                cnt[n.eng] += 1
                n.milestone = (self.eng_sem[n.eng], cnt[n.eng])
            else:
                n.milestone = None
        per = {e: [n for n in nodes if n.eng == e] for e in ENGS}
        nc = self.nc
        with nc.Block() as block:
            def mkbody(ename):
                def body(eng):
                    wm = {}
                    for n in per[ename]:
                        for dd in n.deps:
                            h, v = nodes[dd].milestone
                            k = id(h)
                            if wm.get(k, 0) < v:
                                eng.wait_ge(h, v)
                                wm[k] = v
                        instr = n.fn(eng)
                        if n.slot is not None:
                            instr.then_inc(n.slot.h, 16)
                        elif n.milestone is not None:
                            instr.then_inc(n.milestone[0], 1)
                return body
            for e in ENGS:
                getattr(block, e)(mkbody(e))


def build_program(T=4096, debug=()):
    import contextlib
    nc = bass.Bass("TRN2", target_bir_lowering=False)
    g = G(nc)

    S1 = 512
    G1 = S1 // 128
    NT1 = T // S1
    QB = 512
    NQB = T // QB
    S3 = 256
    G3 = S3 // 128
    NT3 = T // S3

    def dbg(name):
        return "ExternalOutput" if name in debug else "Internal"

    x_in = nc.dram_tensor("x", [T, D], F32, kind="ExternalInput").ap()
    out_hbm = nc.dram_tensor("out", [T, D], F32, kind="ExternalOutput").ap()
    wd = {}
    for nm in ("a_wq", "a_wk", "a_wv", "a_wo", "f_wr", "f_wk", "f_wv"):
        wd[nm] = nc.dram_tensor(nm, [D, D], F32, kind="ExternalInput").ap()
    mur_d = nc.dram_tensor("mu_r", [D], F32, kind="ExternalInput").ap()
    muk_d = nc.dram_tensor("mu_k", [D], F32, kind="ExternalInput").ap()

    S_x1 = nc.dram_tensor("S_x1", [T, D], F32, kind=dbg("S_x1")).ap()
    S_K = nc.dram_tensor("S_K", [D, T], BF16, kind=dbg("S_K")).ap()
    S_V = nc.dram_tensor("S_V", [T, D], BF16, kind=dbg("S_V")).ap()
    S_q = nc.dram_tensor("S_q", [D, T], BF16, kind=dbg("S_q")).ap()
    S_ctx = nc.dram_tensor("S_ctx", [T, D], BF16, kind=dbg("S_ctx")).ap()
    S_x2 = nc.dram_tensor("S_x2", [T, D], F32, kind=dbg("S_x2")).ap()

    idf_d = nc.inline_tensor(np.eye(128, dtype=np.float32), "idf").ap()
    idb_d = nc.inline_tensor(np.eye(128, dtype=ml_dtypes.bfloat16), "idb").ap()

    idf = nc.alloc_sbuf_tensor("idf_sb", [128, 128], F32)
    idb = nc.alloc_sbuf_tensor("idb_sb", [128, 128], BF16)
    mur_tm = nc.alloc_sbuf_tensor("mur_tm", [8, 128], F32)
    muk_tm = nc.alloc_sbuf_tensor("muk_tm", [8, 128], F32)
    mur = nc.alloc_sbuf_tensor("mur_fm", [128, 8], F32)
    muk = nc.alloc_sbuf_tensor("muk_fm", [128, 8], F32)
    lastc = nc.alloc_sbuf_tensor("lastc", [128, 8], BF16)
    epsb = nc.alloc_sbuf_tensor("epsb", [128, 1], F32)
    bn6 = nc.alloc_sbuf_tensor("bn6", [128, 12 * 4], F32)
    mv = [nc.alloc_sbuf_tensor(f"mv{k}", [128, 2 * 4], F32) for k in range(2)]
    vtmp = nc.alloc_sbuf_tensor("vtmp", [128, 4], F32)
    vtmp2 = nc.alloc_sbuf_tensor("vtmp2", [128, 4], F32)
    rstd = [nc.alloc_sbuf_tensor(f"rstd{k}", [128, 4], F32) for k in range(2)]
    wl = [nc.alloc_sbuf_tensor(f"wl_{k}", [128, 1024], BF16) for k in range(4)]

    mm = [nc.alloc_psum_tensor(f"mm{k}", [128, 512], F32) for k in range(2)]
    pt = [nc.alloc_psum_tensor(f"pt{k}", [128, 1024], BF16) for k in range(2)]
    big = [nc.alloc_psum_tensor(f"big{k}", [128, 1024], F32) for k in range(2)]

    mv3 = [m.ap().rearrange("p (gr two) -> p gr two", two=2) for m in mv]

    g.dma("sync", idf[:], idf_d, writes=("IDF",), slot=g.slot("idf"))
    g.dma("sync", idb[:], idb_d, writes=("IDB",), slot=g.slot("idb"))
    g.dma("sync", mur_tm[:], mur_d.rearrange("(c p) -> c p", p=128),
          writes=("MURT",), slot=g.slot("murt"))
    g.dma("sync", muk_tm[:], muk_d.rearrange("(c p) -> c p", p=128),
          writes=("MUKT",), slot=g.slot("mukt"))
    g.node("tensor", lambda e: e.transpose(big[0][:, 0:8], mur_tm[:], idf[0:8, 0:8]),
           reads=("MURT", "IDF"), writes=("BIG0",))
    g.node("vector", lambda e: e.tensor_copy(mur[:], big[0][:, 0:8]),
           reads=("BIG0",), writes=("MUR",))
    g.node("tensor", lambda e: e.transpose(big[1][:, 0:8], muk_tm[:], idf[0:8, 0:8]),
           reads=("MUKT", "IDF"), writes=("BIG1",))
    g.node("vector", lambda e: e.tensor_copy(muk[:], big[1][:, 0:8]),
           reads=("BIG1",), writes=("MUK",))
    g.node("gpsimd", lambda e: e.memset(epsb[:], EPS), writes=("EPSB",))

    def prep_weight(nm, wT, wT_key):
        w3 = wd[nm].rearrange("(a p) i -> a p i", p=128)
        wT3 = wT.ap().rearrange("p (c o) -> p c o", o=1024)
        for oc in range(8):
            wli = wl[oc % 4]
            g.dma("gpsimd", wli[:], w3[oc], writes=(f"WL{oc % 4}",),
                  slot=g.slot(f"WL{oc % 4}"))
            ptk = oc % 2
            def trf(e, wli=wli, ptk=ptk):
                last = None
                for ic in range(8):
                    last = e.transpose(pt[ptk][:, 128 * ic:128 * ic + 128],
                                       wli[:, 128 * ic:128 * ic + 128], idb[:])
                return last
            g.node("tensor", trf, reads=(f"WL{oc % 4}", "IDB"),
                   writes=(f"PT{ptk}",))
            def evf(e, ptk=ptk, oc=oc, wT3=wT3):
                return e.copy(wT3[:, :, 128 * oc:128 * oc + 128],
                              pt[ptk].ap().rearrange("p (c q) -> p c q", q=128))
            g.node("scalar", evf, reads=(f"PT{ptk}",), writes=(wT_key,))

    def ln_pre(xbuf_key, x3, ngroups, sb):
        def bnf(e, x3=x3, ngroups=ngroups):
            last = None
            for gq in range(ngroups):
                for h in range(2):
                    last = e.bn_stats(bn6[:, 12 * gq + 6 * h:12 * gq + 6 * h + 6],
                                      x3[:, gq, 512 * h:512 * h + 512])
            return last
        g.node("vector", bnf, reads=(xbuf_key,), writes=("BN6",))
        def agf(e, ngroups=ngroups, sb=sb):
            last = None
            for gq in range(ngroups):
                last = e.bn_aggr(mv[sb][:, 2 * gq:2 * gq + 2],
                                 bn6[:, 12 * gq:12 * gq + 12])
            return last
        g.node("vector", agf, reads=("BN6",), writes=(f"MV{sb}",))
        g.node("vector", lambda e, n=ngroups, sb=sb: e.tensor_scalar_add(
            vtmp2[:, 0:n], mv3[sb][:, 0:n, 1], EPS),
            reads=(f"MV{sb}",), writes=("VT2",))
        g.node("scalar", lambda e, n=ngroups: e.activation(
            vtmp[:, 0:n], vtmp2[:, 0:n], AF.Ln),
            reads=("VT2",), writes=("VTMP",))
        g.node("scalar", lambda e, n=ngroups, sb=sb: e.activation(
            rstd[sb][:, 0:n], vtmp[:, 0:n], AF.Exp, scale=-0.5),
            reads=("VTMP",), writes=(f"RSTD{sb}",))

    def ln_post(xbuf_key, x3, xn3, ngroups, xn_key, sb):
        def nrm(e, x3=x3, xn3=xn3, ngroups=ngroups, sb=sb):
            last = None
            for gq in range(ngroups):
                last = e.tensor_scalar(xn3[:, gq, :], x3[:, gq, :],
                                       mv3[sb][:, gq, 0:1], rstd[sb][:, gq:gq + 1],
                                       ALU.subtract, ALU.mult)
            return last
        g.node("vector", nrm, reads=(xbuf_key, f"MV{sb}", f"RSTD{sb}"),
               writes=(xn_key,))

    def tm_to_fm(src3, src_key, dst3, dst_key, ngroups, gcol, cnt):
        for gq in range(ngroups):
            a = cnt[0] % 2
            cnt[0] += 1
            ps, pkey = pt[a], f"PT{a}"
            def trf(e, src3=src3, gq=gq, ps=ps):
                last = None
                for c in range(8):
                    last = e.transpose(ps[:, 128 * c:128 * c + 128],
                                       src3[:, gq, 128 * c:128 * c + 128], idb[:])
                return last
            g.node("tensor", trf, reads=(src_key, "IDB"), writes=(pkey,))
            def evf(e, ps=ps, dst3=dst3, gq=gq):
                return e.copy(dst3[:, :, gcol(gq):gcol(gq) + 128],
                              ps.ap().rearrange("p (c q) -> p c q", q=128))
            g.node("scalar", evf, reads=(pkey,), writes=(dst_key,))

    def ffn_mix(S, xnf3, xnf_key, t1, t1_key, xr, xr_key, xk, xk_key):
        g.node("vector", lambda e: e.tensor_sub(
            t1.ap().rearrange("p (c s) -> p c s", s=S),
            xnf3[:, :, 1:S + 1], xnf3[:, :, 0:S]),
            reads=(xnf_key,), writes=(t1_key,))
        def mixf(e, mu, dst, S=S):
            last = None
            t13 = t1.ap().rearrange("p (c s) -> p c s", s=S)
            d3 = dst.ap().rearrange("p (c s) -> p c s", s=S)
            for c in range(8):
                last = e.scalar_tensor_tensor(d3[:, c, :], t13[:, c, :],
                                              mu[:, c:c + 1], xnf3[:, c, 0:S],
                                              ALU.mult, ALU.add)
            return last
        g.node("vector", lambda e: mixf(e, mur, xr), reads=(t1_key, xnf_key, "MUR"),
               writes=(xr_key,))
        g.node("vector", lambda e: mixf(e, muk, xk), reads=(t1_key, xnf_key, "MUK"),
               writes=(xk_key,))

    def proj_node(S, rhs, rhs_keys, wT, wT_key, oc, mmk):
        def pf(e, rhs=rhs, wT=wT, oc=oc, mmk=mmk, S=S):
            last = None
            for ic in range(8):
                last = e.matmul(mm[mmk][:, 0:S],
                                wT[:, 1024 * ic + 128 * oc:1024 * ic + 128 * oc + 128],
                                rhs[:, S * ic:S * ic + S],
                                start=(ic == 0), stop=(ic == 7))
            return last
        g.node("tensor", pf, reads=tuple(rhs_keys) + (wT_key,), writes=(f"MM{mmk}",))

    def ffn_mm(S, xr, xr_key, xk, xk_key, relu_b, relu_pfx, rbuf, r_key,
               x3, xbk, wrT, wrT_k, wkT, wkT_k, wvT, wvT_k, cnt, mid=None):
        M = S // 128
        for m in range(M):
            sl = m % 3
            if sl < 2:
                def rf(e, m=m, sl=sl, xr=xr, wrT=wrT, S=S):
                    last = None
                    for h in range(2):
                        for ic in range(8):
                            last = e.matmul(
                                big[sl][:, 512 * h:512 * h + 512],
                                xr[:, S * ic + 128 * m:S * ic + 128 * m + 128],
                                wrT[:, 1024 * ic + 512 * h:1024 * ic + 512 * h + 512],
                                start=(ic == 0), stop=(ic == 7))
                    return last
                g.node("tensor", rf, reads=(xr_key, wrT_k), writes=(f"BIG{sl}",))
                g.node("scalar", lambda e, m=m, sl=sl: e.activation(
                    rbuf[:, 1024 * m:1024 * m + 1024], big[sl][:], AF.Sigmoid),
                    reads=(f"BIG{sl}",), writes=(r_key,))
            else:
                def rf2(e, m=m, xr=xr, wrT=wrT, S=S):
                    last = None
                    for h in range(2):
                        for ic in range(8):
                            last = e.matmul(
                                mm[h][:, 0:512],
                                xr[:, S * ic + 128 * m:S * ic + 128 * m + 128],
                                wrT[:, 1024 * ic + 512 * h:1024 * ic + 512 * h + 512],
                                start=(ic == 0), stop=(ic == 7))
                    return last
                g.node("tensor", rf2, reads=(xr_key, wrT_k),
                       writes=("MM0", "MM1"))
                def sg2(e, m=m):
                    e.activation(rbuf[:, 1024 * m:1024 * m + 512],
                                 mm[0][:, 0:512], AF.Sigmoid)
                    return e.activation(rbuf[:, 1024 * m + 512:1024 * m + 1024],
                                        mm[1][:, 0:512], AF.Sigmoid)
                g.node("scalar", sg2, reads=("MM0", "MM1"), writes=(r_key,))
        if mid is not None:
            mid()
        rl_keys = tuple(f"{relu_pfx}{oc}" for oc in range(8))
        for oc in range(8):
            k = cnt[0] % 2
            cnt[0] += 1
            proj_node(S, xk, (xk_key,), wkT, wkT_k, oc, k)
            g.node("scalar", lambda e, k=k, oc=oc: e.activation(
                relu_b[:, S * oc:S * oc + S], mm[k][:, 0:S], AF.Relu),
                reads=(f"MM{k}",), writes=(rl_keys[oc],))
            g.node("gpsimd", lambda e, oc=oc: e.tensor_mul(
                relu_b[:, S * oc:S * oc + S], relu_b[:, S * oc:S * oc + S],
                relu_b[:, S * oc:S * oc + S]),
                reads=(rl_keys[oc],), writes=(rl_keys[oc],))
        for m in range(M):
            sl = m % 3
            if sl < 2:
                def vf(e, m=m, sl=sl, relu_b=relu_b, wvT=wvT, S=S):
                    last = None
                    for h in range(2):
                        for ic in range(8):
                            last = e.matmul(
                                big[sl][:, 512 * h:512 * h + 512],
                                relu_b[:, S * ic + 128 * m:S * ic + 128 * m + 128],
                                wvT[:, 1024 * ic + 512 * h:1024 * ic + 512 * h + 512],
                                start=(ic == 0), stop=(ic == 7))
                    return last
                g.node("tensor", vf, reads=rl_keys + (wvT_k,),
                       writes=(f"BIG{sl}",))
                g.node("vector", lambda e, m=m, sl=sl: e.tensor_mul(
                    big[sl][:], big[sl][:], rbuf[:, 1024 * m:1024 * m + 1024]),
                    reads=(f"BIG{sl}", r_key), writes=(f"BIG{sl}",))
                g.node("vector", lambda e, m=m, sl=sl, x3=x3: e.scalar_tensor_tensor(
                    x3[:, m, :], big[sl][:], 0.5, x3[:, m, :], ALU.mult, ALU.add),
                    reads=(f"BIG{sl}", xbk), writes=(xbk,))
            else:
                def vf2(e, m=m, relu_b=relu_b, wvT=wvT, S=S):
                    last = None
                    for h in range(2):
                        for ic in range(8):
                            last = e.matmul(
                                mm[h][:, 0:512],
                                relu_b[:, S * ic + 128 * m:S * ic + 128 * m + 128],
                                wvT[:, 1024 * ic + 512 * h:1024 * ic + 512 * h + 512],
                                start=(ic == 0), stop=(ic == 7))
                    return last
                g.node("tensor", vf2, reads=rl_keys + (wvT_k,),
                       writes=("MM0", "MM1"))
                def gt2(e, m=m):
                    e.tensor_mul(mm[0][:, 0:512], mm[0][:, 0:512],
                                 rbuf[:, 1024 * m:1024 * m + 512])
                    return e.tensor_mul(mm[1][:, 0:512], mm[1][:, 0:512],
                                        rbuf[:, 1024 * m + 512:1024 * m + 1024])
                g.node("vector", gt2, reads=("MM0", "MM1", r_key),
                       writes=("MM0", "MM1"))
                def st2(e, m=m, x3=x3):
                    e.scalar_tensor_tensor(x3[:, m, 0:512], mm[0][:, 0:512],
                                           0.5, x3[:, m, 0:512],
                                           ALU.mult, ALU.add)
                    return e.scalar_tensor_tensor(
                        x3[:, m, 512:1024], mm[1][:, 0:512], 0.5,
                        x3[:, m, 512:1024], ALU.mult, ALU.add)
                g.node("vector", st2, reads=("MM0", "MM1", xbk),
                       writes=(xbk,))

    st = contextlib.ExitStack()
    wrT = st.enter_context(nc.sbuf_tensor("wrT", [128, 8 * 1024], BF16))
    wkT = st.enter_context(nc.sbuf_tensor("wkT", [128, 8 * 1024], BF16))
    wvT = st.enter_context(nc.sbuf_tensor("wvT", [128, 8 * 1024], BF16))
    x01 = [st.enter_context(nc.sbuf_tensor(f"x01_{k}", [128, G1 * 1024], F32))
           for k in range(4)]
    xn_tm = st.enter_context(nc.sbuf_tensor("xn_tm", [128, G1 * 1024], BF16))
    xnf = st.enter_context(nc.sbuf_tensor("xnf", [128, 8 * (S1 + 1)], BF16))
    t1b = st.enter_context(nc.sbuf_tensor("t1b", [128, 8 * S1], BF16))
    xrb = [st.enter_context(nc.sbuf_tensor(f"xrb_{k}", [128, 8 * S1], BF16))
           for k in range(2)]
    xkb = [st.enter_context(nc.sbuf_tensor(f"xkb_{k}", [128, 8 * S1], BF16))
           for k in range(2)]
    relu1 = st.enter_context(nc.sbuf_tensor("relu1", [128, 8 * S1], BF16))
    rb = st.enter_context(nc.sbuf_tensor("rb", [128, 8 * S1], F32))

    x_t = x_in.rearrange("(n g p) d -> n p g d", g=G1, p=128)
    sx1_t = S_x1.rearrange("(n g p) d -> n p g d", g=G1, p=128)
    xnf3 = xnf.ap().rearrange("p (c s) -> p c s", s=S1 + 1)
    cnt1 = [0]

    def p1a_load(i):
        xbk = f"X01_{i % 4}"
        x3 = x01[i % 4].ap().rearrange("p (gr d) -> p gr d", d=1024)
        g.dma("sync", x3, x_t[i], writes=(xbk,), slot=g.slot(xbk))

    def p1a_stats(i):
        xbk = f"X01_{i % 4}"
        x3 = x01[i % 4].ap().rearrange("p (gr d) -> p gr d", d=1024)
        ln_pre(xbk, x3, G1, i % 2)

    def p1a_nrm(i):
        xbk = f"X01_{i % 4}"
        x3 = x01[i % 4].ap().rearrange("p (gr d) -> p gr d", d=1024)
        xn3 = xn_tm.ap().rearrange("p (gr d) -> p gr d", d=1024)
        ln_post(xbk, x3, xn3, G1, "XN", i % 2)

    def p1a_fm(i):
        xn3 = xn_tm.ap().rearrange("p (gr d) -> p gr d", d=1024)
        if i == 0:
            g.node("gpsimd", lambda e: e.memset(xnf3[:, :, 0], 0.0),
                   writes=("XNF",))
        else:
            g.node("gpsimd", lambda e: e.tensor_copy(xnf3[:, :, 0], lastc[:, 0:8]),
                   reads=("LASTC",), writes=("XNF",))
        tm_to_fm(xn3, "XN", xnf3, "XNF", G1, lambda gq: 1 + 128 * gq, cnt1)
        g.node("gpsimd", lambda e: e.tensor_copy(lastc[:, 0:8], xnf3[:, :, S1]),
               reads=("XNF",), writes=("LASTC",))
        ffn_mix(S1, xnf3, "XNF", t1b, "T1", xrb[i % 2], f"XR{i % 2}",
                xkb[i % 2], f"XK{i % 2}")

    def p1a_B1(i):
        def mid(i=i):
            if i + 2 < NT1:
                p1a_stats(i + 2)
            if i + 1 < NT1:
                p1a_fm(i + 1)
        xbk = f"X01_{i % 4}"
        x3 = x01[i % 4].ap().rearrange("p (gr d) -> p gr d", d=1024)
        ffn_mm(S1, xrb[i % 2], f"XR{i % 2}", xkb[i % 2], f"XK{i % 2}",
               relu1, "RL1_", rb, "RB", x3, xbk,
               wrT, "WRT", wkT, "WKT", wvT, "WVT", cnt1, mid=mid)

    def p1a_B2(i):
        xbk = f"X01_{i % 4}"
        x3 = x01[i % 4].ap().rearrange("p (gr d) -> p gr d", d=1024)
        g.dma("gpsimd", sx1_t[i], x3, reads=(xbk,), writes=("RG_X1",),
              slot=g.slot(f"x1st_{i % 4}"))

    prep_weight("f_wr", wrT, "WRT")
    p1a_load(0)
    if NT1 > 1:
        p1a_load(1)
    p1a_stats(0)
    p1a_nrm(0)
    p1a_fm(0)
    prep_weight("f_wk", wkT, "WKT")
    if NT1 > 2:
        p1a_load(2)
    if NT1 > 1:
        p1a_stats(1)
        p1a_nrm(1)
    prep_weight("f_wv", wvT, "WVT")
    if NT1 > 3:
        p1a_load(3)
    for i in range(NT1):
        p1a_B1(i)
        p1a_B2(i)
        if i + 4 < NT1:
            p1a_load(i + 4)
        if i + 2 < NT1:
            p1a_nrm(i + 2)

    g.barrier()
    st.close()

    st = contextlib.ExitStack()
    wqT = st.enter_context(nc.sbuf_tensor("wqT", [128, 8 * 1024], BF16))
    wkTa = st.enter_context(nc.sbuf_tensor("wkTa", [128, 8 * 1024], BF16))
    wvTa = st.enter_context(nc.sbuf_tensor("wvTa", [128, 8 * 1024], BF16))
    x1t = [st.enter_context(nc.sbuf_tensor(f"x1t_{k}", [128, G1 * 1024], F32))
           for k in range(4)]
    xn2b = st.enter_context(nc.sbuf_tensor("xn2b", [128, G1 * 1024], BF16))
    xn2f = [st.enter_context(nc.sbuf_tensor(f"xn2f_{k}", [128, 8 * S1], BF16))
            for k in range(2)]
    qb = st.enter_context(nc.sbuf_tensor("qb", [128, 8 * S1], BF16))
    kbb = st.enter_context(nc.sbuf_tensor("kbb", [128, 8 * S1], BF16))
    vbb = st.enter_context(nc.sbuf_tensor("vbb", [128, G1 * 1024], BF16))

    sq_t = S_q.rearrange("(c p) t -> p c t", p=128)
    sk_t = S_K.rearrange("(c p) t -> p c t", p=128)
    sv_t = S_V.rearrange("(n g p) d -> n p g d", g=G1, p=128)
    cnt2 = [0]

    def p1b_load(i):
        xbk = f"X1T_{i % 4}"
        x3 = x1t[i % 4].ap().rearrange("p (gr d) -> p gr d", d=1024)
        g.dma("sync", x3, sx1_t[i], reads=("RG_X1",), writes=(xbk,),
              slot=g.slot(xbk))

    def p1b_stats(i):
        xbk = f"X1T_{i % 4}"
        x3 = x1t[i % 4].ap().rearrange("p (gr d) -> p gr d", d=1024)
        ln_pre(xbk, x3, G1, i % 2)

    def p1b_post(i):
        xb = x1t[i % 4]
        xbk = f"X1T_{i % 4}"
        x3 = xb.ap().rearrange("p (gr d) -> p gr d", d=1024)
        xn3 = xn2b.ap().rearrange("p (gr d) -> p gr d", d=1024)
        ln_post(xbk, x3, xn3, G1, "XN2B", i % 2)
        xn2f3 = xn2f[i % 2].ap().rearrange("p (c s) -> p c s", s=S1)
        tm_to_fm(xn3, "XN2B", xn2f3, f"XN2F{i % 2}", G1,
                 lambda gq: 128 * gq, cnt2)

    def p1b_B(i, mid=None):
        xf = xn2f[i % 2]
        xfk = f"XN2F{i % 2}"
        for dst, dkey, wT, wkey in ((qb, "QB", wqT, "AWQ"), (kbb, "KB", wkTa, "AWK")):
            for oc in range(8):
                k = cnt2[0] % 2
                cnt2[0] += 1
                proj_node(S1, xf, (xfk,), wT, wkey, oc, k)
                g.node("scalar", lambda e, k=k, oc=oc, dst=dst: e.copy(
                    dst[:, S1 * oc:S1 * oc + S1], mm[k][:, 0:S1]),
                    reads=(f"MM{k}",), writes=(dkey,))
        if mid is not None:
            mid()
        vb3 = vbb.ap().rearrange("p (gr d) -> p gr d", d=1024)
        for gq in range(G1):
            sl = gq % 3
            if sl < 2:
                def pvf(e, gq=gq, sl=sl, xf=xf):
                    last = None
                    for h in range(2):
                        for ic in range(8):
                            last = e.matmul(
                                big[sl][:, 512 * h:512 * h + 512],
                                xf[:, S1 * ic + 128 * gq:S1 * ic + 128 * gq + 128],
                                wvTa[:, 1024 * ic + 512 * h:1024 * ic + 512 * h + 512],
                                start=(ic == 0), stop=(ic == 7))
                    return last
                g.node("tensor", pvf, reads=(xfk, "AWV"), writes=(f"BIG{sl}",))
                g.node("scalar", lambda e, gq=gq, sl=sl, vb3=vb3: e.copy(
                    vb3[:, gq, :], big[sl][:]),
                    reads=(f"BIG{sl}",), writes=("VB",))
            else:
                def pvf2(e, gq=gq, xf=xf):
                    last = None
                    for h in range(2):
                        for ic in range(8):
                            last = e.matmul(
                                mm[h][:, 0:512],
                                xf[:, S1 * ic + 128 * gq:S1 * ic + 128 * gq + 128],
                                wvTa[:, 1024 * ic + 512 * h:1024 * ic + 512 * h + 512],
                                start=(ic == 0), stop=(ic == 7))
                    return last
                g.node("tensor", pvf2, reads=(xfk, "AWV"),
                       writes=("MM0", "MM1"))
                def cpv2(e, gq=gq, vb3=vb3):
                    e.copy(vb3[:, gq, 0:512], mm[0][:, 0:512])
                    return e.copy(vb3[:, gq, 512:1024], mm[1][:, 0:512])
                g.node("scalar", cpv2, reads=("MM0", "MM1"), writes=("VB",))
        qb3 = qb.ap().rearrange("p (c s) -> p c s", s=S1)
        kb3 = kbb.ap().rearrange("p (c s) -> p c s", s=S1)
        g.dma("gpsimd", sq_t[:, :, S1 * i:S1 * i + S1], qb3, reads=("QB",),
              writes=("RG_q",), slot=g.slot("qbst"))
        g.dma("gpsimd", sk_t[:, :, S1 * i:S1 * i + S1], kb3, reads=("KB",),
              writes=("RG_K",), slot=g.slot("kbst"))
        g.dma("gpsimd", sv_t[i], vb3, reads=("VB",), writes=("RG_V",),
              slot=g.slot("vbst"))

    prep_weight("a_wq", wqT, "AWQ")
    p1b_load(0)
    if NT1 > 1:
        p1b_load(1)
    p1b_stats(0)
    p1b_post(0)
    prep_weight("a_wk", wkTa, "AWK")
    if NT1 > 2:
        p1b_load(2)
    if NT1 > 1:
        p1b_stats(1)
    prep_weight("a_wv", wvTa, "AWV")
    for i in range(NT1):
        mid = (lambda i2=i + 2: p1b_stats(i2)) if i + 2 < NT1 else None
        p1b_B(i, mid=mid)
        if i + 3 < NT1:
            p1b_load(i + 3)
        if i + 1 < NT1:
            p1b_post(i + 1)

    g.barrier()
    st.close()

    st = contextlib.ExitStack()
    qblk = [st.enter_context(nc.sbuf_tensor(f"qblk_{k}", [128, 8 * 512], BF16))
            for k in range(2)]
    Kb = [st.enter_context(nc.sbuf_tensor(f"Kb_{k}", [128, 8 * 512], BF16))
          for k in range(4)]
    Vb = [st.enter_context(nc.sbuf_tensor(f"Vb_{k}", [128, 4 * 512], BF16))
          for k in range(3)]
    srow = [st.enter_context(nc.sbuf_tensor(f"srow_{s}", [128, T], F32))
            for s in range(4)]
    prow = [st.enter_context(nc.sbuf_tensor(f"prow_{s}", [128, T], BF16))
            for s in range(4)]
    pTs4 = [[st.enter_context(nc.sbuf_tensor(f"pTs_{s}_{j}", [128, 512], BF16))
             for j in range(NQB)] for s in range(4)]
    ctxsb = st.enter_context(nc.sbuf_tensor("ctxsb", [128, 4 * 1024], BF16))
    masks = st.enter_context(nc.sbuf_tensor("masks", [128, 512], BF16))
    nmx = [st.enter_context(nc.sbuf_tensor(f"nmx_{k}", [128, 4], F32))
           for k in range(2)]
    cmx = [st.enter_context(nc.sbuf_tensor(f"cmx_{k}", [128, 4 * 8], F32))
           for k in range(2)]
    lsm = [st.enter_context(nc.sbuf_tensor(f"lsm_{k}", [128, 4], F32))
           for k in range(2)]
    lsmh = [st.enter_context(nc.sbuf_tensor(f"lsmh_{k}", [128, 4], F32))
            for k in range(2)]
    liv = [st.enter_context(nc.sbuf_tensor(f"liv_{k}", [128, 4], F32))
           for k in range(2)]
    sctx_t = S_ctx.rearrange("(n g p) d -> n p g d", g=4, p=128)

    g.node("gpsimd", lambda e: e.memset(masks[:], 0.0), writes=("MASKS",))
    def mfn(e):
        return e.affine_select(masks[:], masks[:], [[-1, 512]], ALU.is_ge,
                               -1e30, base=384, channel_multiplier=1)
    g.node("gpsimd", mfn, reads=("MASKS",), writes=("MASKS",))

    pcnt = [0]

    def p2_A(b):
        par = b % 2
        qk = f"QBLK_{par}"
        g.dma("sync", qblk[par].ap().rearrange("p (c s) -> p c s", s=512),
              sq_t[:, :, QB * b:QB * b + QB], reads=("RG_q",), writes=(qk,),
              slot=g.slot(qk))
        for kbi in range(b + 1):
            kkey = f"KB_{kbi % 4}"
            g.dma("sync", Kb[kbi % 4].ap().rearrange("p (c s) -> p c s", s=512),
                  sk_t[:, :, 512 * kbi:512 * kbi + 512], reads=("RG_K",),
                  writes=(kkey,), slot=g.slot(kkey))
            for s in range(4):
                k = pcnt[0] % 2
                pcnt[0] += 1
                N = 128 * (s + 1) if kbi == b else 512
                def sf(e, k=k, s=s, kbi=kbi, par=par, N=N):
                    last = None
                    qh = qblk[par]
                    for ic in range(8):
                        last = e.matmul(
                            mm[k][:, 0:N],
                            qh[:, 512 * ic + 128 * s:512 * ic + 128 * s + 128],
                            Kb[kbi % 4][:, 512 * ic:512 * ic + N],
                            start=(ic == 0), stop=(ic == 7))
                    return last
                g.node("tensor", sf, reads=(qk, kkey), writes=(f"MM{k}",))
                if s % 2 == 0 and kbi >= 3:
                    g.node("scalar", lambda e, k=k, s=s, kbi=kbi, N=N: e.activation(
                        srow[s][:, 512 * kbi:512 * kbi + N], mm[k][:, 0:N],
                        AF.Copy, scale=0.03125),
                        reads=(f"MM{k}",), writes=(f"SROW{s}",))
                else:
                    g.node("vector", lambda e, k=k, s=s, kbi=kbi, N=N: e.tensor_scalar_mul(
                        srow[s][:, 512 * kbi:512 * kbi + N], mm[k][:, 0:N],
                        0.03125),
                        reads=(f"MM{k}",), writes=(f"SROW{s}",))
                if kbi == b:
                    off = 384 - 128 * s
                    g.node("vector", lambda e, s=s, b=b, off=off, N=N: e.tensor_add(
                        srow[s][:, 512 * b:512 * b + N],
                        srow[s][:, 512 * b:512 * b + N],
                        masks[:, off:off + N]),
                        reads=(f"SROW{s}", "MASKS"), writes=(f"SROW{s}",))
                g.node("vector", lambda e, s=s, kbi=kbi, par=par, N=N: e.reduce_max(
                    cmx[par][:, 8 * s + kbi:8 * s + kbi + 1],
                    srow[s][:, 512 * kbi:512 * kbi + N], AX.X),
                    reads=(f"SROW{s}",), writes=(f"CMX{par}",))
        n = 512 * (b + 1)
        for s in range(4):
            g.node("vector", lambda e, s=s, b=b, par=par: e.reduce_max(
                nmx[par][:, s:s + 1], cmx[par][:, 8 * s:8 * s + b + 1], AX.X,
                negate=True),
                reads=(f"CMX{par}",), writes=(f"NMX{par}",))

    def p2_Aexp(b):
        par = b % 2
        n = 512 * (b + 1)
        nh = (n // 1024) * 512 if n >= 1024 else n
        for s in range(4):
            ne = min(nh, 512 * b + 128 * (s + 1))
            g.node("scalar", lambda e, s=s, ne=ne, par=par: e.activation(
                prow[s][:, 0:ne], srow[s][:, 0:ne], AF.Exp,
                bias=nmx[par][:, s:s + 1], accum_out=lsmh[0][:, s:s + 1]),
                reads=(f"SROW{s}", f"NMX{par}"),
                writes=(f"PROW_{s}_0", "LSMH0"))
        if nh < n:
            for s in range(4):
                ns_ = 512 * b + 128 * (s + 1)
                g.node("scalar", lambda e, s=s, nh=nh, ns_=ns_, par=par: e.activation(
                    prow[s][:, nh:ns_], srow[s][:, nh:ns_], AF.Exp,
                    bias=nmx[par][:, s:s + 1], accum_out=lsmh[1][:, s:s + 1]),
                    reads=(f"SROW{s}", f"NMX{par}"),
                    writes=(f"PROW_{s}_1", "LSMH1"))
            g.node("vector", lambda e, par=par: e.tensor_add(
                lsm[par][:, 0:4], lsmh[0][:, 0:4], lsmh[1][:, 0:4]),
                reads=("LSMH0", "LSMH1"), writes=(f"LSM{par}",))
        else:
            g.node("vector", lambda e, par=par: e.tensor_copy(
                lsm[par][:, 0:4], lsmh[0][:, 0:4]),
                reads=("LSMH0",), writes=(f"LSM{par}",))
        for s in range(4):
            g.node("vector", lambda e, s=s, par=par: e.reciprocal(
                liv[par][:, s:s + 1], lsm[par][:, s:s + 1]),
                reads=(f"LSM{par}",), writes=(f"LIV{par}",))

    def p2_prefV(b):
        for kbi in range(min(3, b + 1)):
            vkey = f"VB_{kbi % 3}"
            g.dma("gpsimd",
                  Vb[kbi % 3].ap().rearrange("p (gr d) -> p gr d", d=512),
                  sv_t[kbi][:, :, 0:512], reads=("RG_V",), writes=(vkey,),
                  slot=g.slot(vkey))

    ACCK = ("B0L", "B0R", "B1L", "B1R")

    def acc_ap(s):
        return big[s // 2][:, 512 * (s % 2):512 * (s % 2) + 512]

    def p2_Bhalf(b, fh):
        par = b % 2
        n = 512 * (b + 1)
        nh = (n // 1024) * 512 if n >= 1024 else n
        for kbi in range(b + 1):
            vkey = f"VB_{kbi % 3}"
            if fh == 1 or kbi >= min(3, b + 1):
                g.dma("gpsimd",
                      Vb[kbi % 3].ap().rearrange("p (gr d) -> p gr d", d=512),
                      sv_t[kbi][:, :, 512 * fh:512 * fh + 512],
                      reads=("RG_V",), writes=(vkey,), slot=g.slot(vkey))
            half = 0 if 512 * (kbi + 1) <= nh else 1
            ptk = tuple(f"PTS{s}_{kbi}" for s in range(4))
            if fh == 0:
                for sp in range(2):
                    a = pcnt[0] % 2
                    pcnt[0] += 1
                    def trf(e, sp=sp, kbi=kbi, b=b, a=a):
                        last = None
                        for si in range(2):
                            s = 2 * sp + si
                            jmax = s if kbi == b else 3
                            for j in range(jmax + 1):
                                last = e.transpose(
                                    pt[a][:, 512 * si + 128 * j:512 * si + 128 * j + 128],
                                    prow[s][:, 512 * kbi + 128 * j:512 * kbi + 128 * j + 128],
                                    idb[:])
                        return last
                    g.node("tensor", trf,
                           reads=(f"PROW_{2 * sp}_{half}",
                                  f"PROW_{2 * sp + 1}_{half}", "IDB"),
                           writes=(f"PT{a}",))
                    def cpf(e, sp=sp, kbi=kbi, b=b, a=a):
                        last = None
                        for si in range(2):
                            s = 2 * sp + si
                            jmax = s if kbi == b else 3
                            last = e.copy(
                                pTs4[s][kbi][:, 0:128 * (jmax + 1)],
                                pt[a][:, 512 * si:512 * si + 128 * (jmax + 1)])
                        return last
                    g.node("scalar", cpf, reads=(f"PT{a}",),
                           writes=(ptk[2 * sp], ptk[2 * sp + 1]))
            def avf(e, kbi=kbi, b=b):
                last = None
                vb3 = Vb[kbi % 3].ap().rearrange("p (gr d) -> p gr d", d=512)
                for s in range(4):
                    jmax = s if kbi == b else 3
                    for j in range(jmax + 1):
                        last = e.matmul(
                            acc_ap(s),
                            pTs4[s][kbi][:, 128 * j:128 * j + 128],
                            vb3[:, j, :],
                            start=(kbi == 0 and j == 0),
                            stop=(kbi == b and j == jmax))
                return last
            g.node("tensor", avf, reads=ptk + (vkey,), writes=ACCK)
        for s in range(4):
            g.node("vector", lambda e, s=s, fh=fh, par=par: e.tensor_scalar(
                ctxsb[:, 1024 * s + 512 * fh:1024 * s + 512 * fh + 512],
                acc_ap(s), liv[par][:, s:s + 1], None, ALU.mult),
                reads=(ACCK[s], f"LIV{par}"), writes=("CTXSB",))

    def p2_Bstore(b):
        g.dma("gpsimd", sctx_t[b],
              ctxsb.ap().rearrange("p (gr d) -> p gr d", d=1024),
              reads=("CTXSB",), writes=("RG_ctx",), slot=g.slot("ctxst"))

    p2_A(0)
    p2_Aexp(0)
    for b in range(NQB):
        if b + 1 < NQB:
            p2_A(b + 1)
        p2_prefV(b)
        p2_Bhalf(b, 0)
        if b + 1 < NQB:
            p2_Aexp(b + 1)
        p2_Bhalf(b, 1)
        p2_Bstore(b)

    g.barrier()
    st.close()

    st = contextlib.ExitStack()
    woT = st.enter_context(nc.sbuf_tensor("woT", [128, 8 * 1024], BF16))
    x1c = [st.enter_context(nc.sbuf_tensor(f"x1c_{k}", [128, G1 * 1024], F32))
           for k in range(3)]
    ctxt = [st.enter_context(nc.sbuf_tensor(f"ctxt_{k}", [128, G1 * 1024], BF16))
            for k in range(3)]
    ctxf = [st.enter_context(nc.sbuf_tensor(f"ctxf_{k}", [128, 8 * S1], BF16))
            for k in range(3)]

    sx2_t = S_x2.rearrange("(n g p) d -> n p g d", g=G1, p=128)
    sctx_t1 = S_ctx.rearrange("(n g p) d -> n p g d", g=G1, p=128)
    cnt4 = [0]

    def p3a_A(i):
        ck_ = f"CTXT_{i % 3}"
        c3 = ctxt[i % 3].ap().rearrange("p (gr d) -> p gr d", d=1024)
        g.dma("sync", c3, sctx_t1[i], reads=("RG_ctx",), writes=(ck_,),
              slot=g.slot(ck_))
        ctxf3 = ctxf[i % 3].ap().rearrange("p (c s) -> p c s", s=S1)
        tm_to_fm(c3, ck_, ctxf3, f"CTXF{i % 3}", G1, lambda gq: 128 * gq, cnt4)

    def p3a_B(i):
        xk_ = f"X1C_{i % 3}"
        x3 = x1c[i % 3].ap().rearrange("p (gr d) -> p gr d", d=1024)
        g.dma("sync", x3, sx1_t[i], reads=("RG_X1",), writes=(xk_,),
              slot=g.slot(xk_))
        cf = ctxf[i % 3]
        cfk = f"CTXF{i % 3}"
        for gq in range(G1):
            sl = gq % 3
            if sl < 2:
                def pwf(e, gq=gq, sl=sl, cf=cf):
                    last = None
                    for h in range(2):
                        for ic in range(8):
                            last = e.matmul(
                                big[sl][:, 512 * h:512 * h + 512],
                                cf[:, S1 * ic + 128 * gq:S1 * ic + 128 * gq + 128],
                                woT[:, 1024 * ic + 512 * h:1024 * ic + 512 * h + 512],
                                start=(ic == 0), stop=(ic == 7))
                    return last
                g.node("tensor", pwf, reads=(cfk, "AWO"), writes=(f"BIG{sl}",))
                g.node("vector", lambda e, gq=gq, sl=sl, x3=x3: e.tensor_add(
                    x3[:, gq, :], big[sl][:], x3[:, gq, :]),
                    reads=(f"BIG{sl}", xk_), writes=(xk_,))
            else:
                def pwf2(e, gq=gq, cf=cf):
                    last = None
                    for h in range(2):
                        for ic in range(8):
                            last = e.matmul(
                                mm[h][:, 0:512],
                                cf[:, S1 * ic + 128 * gq:S1 * ic + 128 * gq + 128],
                                woT[:, 1024 * ic + 512 * h:1024 * ic + 512 * h + 512],
                                start=(ic == 0), stop=(ic == 7))
                    return last
                g.node("tensor", pwf2, reads=(cfk, "AWO"),
                       writes=("MM0", "MM1"))
                def ad2(e, gq=gq, x3=x3):
                    e.tensor_add(x3[:, gq, 0:512], mm[0][:, 0:512],
                                 x3[:, gq, 0:512])
                    return e.tensor_add(x3[:, gq, 512:1024], mm[1][:, 0:512],
                                        x3[:, gq, 512:1024])
                g.node("vector", ad2, reads=("MM0", "MM1", xk_),
                       writes=(xk_,))
        g.dma("gpsimd", sx2_t[i], x3, reads=(xk_,), writes=("RG_X2",),
              slot=g.slot(f"x2st_{i % 3}"))

    prep_weight("a_wo", woT, "AWO")
    p3a_A(0)
    if NT1 > 1:
        p3a_A(1)
    for i in range(NT1):
        p3a_B(i)
        if i + 2 < NT1:
            p3a_A(i + 2)

    g.barrier()
    st.close()

    st = contextlib.ExitStack()
    wrT3 = st.enter_context(nc.sbuf_tensor("wrT3", [128, 8 * 1024], BF16))
    wkT3 = st.enter_context(nc.sbuf_tensor("wkT3", [128, 8 * 1024], BF16))
    wvT3 = st.enter_context(nc.sbuf_tensor("wvT3", [128, 8 * 1024], BF16))
    x2c = [st.enter_context(nc.sbuf_tensor(f"x2c_{k}", [128, G1 * 1024], F32))
           for k in range(4)]
    xn3_tm = st.enter_context(nc.sbuf_tensor("xn3_tm", [128, G1 * 1024], BF16))
    xnf_3 = st.enter_context(nc.sbuf_tensor("xnf3", [128, 8 * (S1 + 1)], BF16))
    t1c = st.enter_context(nc.sbuf_tensor("t1c", [128, 8 * S1], BF16))
    xrc = [st.enter_context(nc.sbuf_tensor(f"xrc_{k}", [128, 8 * S1], BF16))
           for k in range(2)]
    xkc = [st.enter_context(nc.sbuf_tensor(f"xkc_{k}", [128, 8 * S1], BF16))
           for k in range(2)]
    relu3 = st.enter_context(nc.sbuf_tensor("relu3", [128, 8 * S1], BF16))
    rc = st.enter_context(nc.sbuf_tensor("rc", [128, 8 * S1], F32))

    out_t1 = out_hbm.rearrange("(n g p) d -> n p g d", g=G1, p=128)
    xnf33 = xnf_3.ap().rearrange("p (c s) -> p c s", s=S1 + 1)
    cnt3 = [0]

    def p3b_load(i):
        xbk = f"X2C_{i % 4}"
        x3 = x2c[i % 4].ap().rearrange("p (gr d) -> p gr d", d=1024)
        g.dma("sync", x3, sx2_t[i], reads=("RG_X2",), writes=(xbk,),
              slot=g.slot(xbk))

    def p3b_stats(i):
        xbk = f"X2C_{i % 4}"
        x3 = x2c[i % 4].ap().rearrange("p (gr d) -> p gr d", d=1024)
        ln_pre(xbk, x3, G1, i % 2)

    def p3b_nrm(i):
        xbk = f"X2C_{i % 4}"
        x3 = x2c[i % 4].ap().rearrange("p (gr d) -> p gr d", d=1024)
        xn3 = xn3_tm.ap().rearrange("p (gr d) -> p gr d", d=1024)
        ln_post(xbk, x3, xn3, G1, "XN3B", i % 2)

    def p3b_fm(i):
        xn3 = xn3_tm.ap().rearrange("p (gr d) -> p gr d", d=1024)
        if i == 0:
            g.node("gpsimd", lambda e: e.memset(xnf33[:, :, 0], 0.0),
                   writes=("XNF3",))
        else:
            g.node("gpsimd", lambda e: e.tensor_copy(xnf33[:, :, 0], lastc[:, 0:8]),
                   reads=("LASTC",), writes=("XNF3",))
        tm_to_fm(xn3, "XN3B", xnf33, "XNF3", G1, lambda gq: 1 + 128 * gq, cnt3)
        g.node("gpsimd", lambda e: e.tensor_copy(lastc[:, 0:8], xnf33[:, :, S1]),
               reads=("XNF3",), writes=("LASTC",))
        ffn_mix(S1, xnf33, "XNF3", t1c, "T1c", xrc[i % 2], f"XRc{i % 2}",
                xkc[i % 2], f"XKc{i % 2}")

    def p3b_B1(i):
        def mid(i=i):
            if i + 2 < NT1:
                p3b_stats(i + 2)
            if i + 1 < NT1:
                p3b_fm(i + 1)
        xbk = f"X2C_{i % 4}"
        x3 = x2c[i % 4].ap().rearrange("p (gr d) -> p gr d", d=1024)
        ffn_mm(S1, xrc[i % 2], f"XRc{i % 2}", xkc[i % 2], f"XKc{i % 2}",
               relu3, "RL3_", rc, "RBc", x3, xbk,
               wrT3, "WRT3", wkT3, "WKT3", wvT3, "WVT3", cnt3, mid=mid)

    def p3b_B2(i):
        xbk = f"X2C_{i % 4}"
        x3 = x2c[i % 4].ap().rearrange("p (gr d) -> p gr d", d=1024)
        g.dma("gpsimd", out_t1[i], x3, reads=(xbk,), writes=("RG_OUT",),
              slot=g.slot(f"outst_{i % 4}"))

    prep_weight("f_wr", wrT3, "WRT3")
    p3b_load(0)
    if NT1 > 1:
        p3b_load(1)
    p3b_stats(0)
    p3b_nrm(0)
    p3b_fm(0)
    prep_weight("f_wk", wkT3, "WKT3")
    if NT1 > 2:
        p3b_load(2)
    if NT1 > 1:
        p3b_stats(1)
        p3b_nrm(1)
    prep_weight("f_wv", wvT3, "WVT3")
    if NT1 > 3:
        p3b_load(3)
    for i in range(NT1):
        p3b_B1(i)
        p3b_B2(i)
        if i + 4 < NT1:
            p3b_load(i + 4)
        if i + 2 < NT1:
            p3b_nrm(i + 2)

    g.barrier()
    st.close()
    g.emit()
    return nc


_CACHE = {}

def _get_program(T=4096, debug=()):
    key = (T, tuple(sorted(debug)))
    if key not in _CACHE:
        _CACHE[key] = build_program(T, debug)
    return _CACHE[key]


def kernel(L0_attn_in_features, L0_attn_w_q, L0_attn_w_k, L0_attn_w_v,
           L0_attn_w_o, L0_ffn_W_r, L0_ffn_W_k, L0_ffn_W_v, L0_ffn_mu_r,
           L0_ffn_mu_k):
    x = np.ascontiguousarray(np.asarray(L0_attn_in_features, dtype=np.float32))
    B, T, d = x.shape
    nc = _get_program(T)
    common = {
        "a_wq": np.ascontiguousarray(np.asarray(L0_attn_w_q, np.float32)),
        "a_wk": np.ascontiguousarray(np.asarray(L0_attn_w_k, np.float32)),
        "a_wv": np.ascontiguousarray(np.asarray(L0_attn_w_v, np.float32)),
        "a_wo": np.ascontiguousarray(np.asarray(L0_attn_w_o, np.float32)),
        "f_wr": np.ascontiguousarray(np.asarray(L0_ffn_W_r, np.float32)),
        "f_wk": np.ascontiguousarray(np.asarray(L0_ffn_W_k, np.float32)),
        "f_wv": np.ascontiguousarray(np.asarray(L0_ffn_W_v, np.float32)),
        "mu_r": np.ascontiguousarray(np.asarray(L0_ffn_mu_r, np.float32)),
        "mu_k": np.ascontiguousarray(np.asarray(L0_ffn_mu_k, np.float32)),
    }
    in_maps = [dict(common, x=np.ascontiguousarray(x[c])) for c in range(B)]
    res = run_bass_kernel_spmd(nc, in_maps, core_ids=list(range(B)))
    return np.stack([np.asarray(res.results[c]["out"], np.float32)
                     for c in range(B)], axis=0)


# version 48
# speedup vs baseline: 1.0063x; 1.0063x over previous
import numpy as np
import ml_dtypes
import concourse.bass as bass
import concourse.mybir as mybir
from concourse.bass_utils import run_bass_kernel_spmd

F32 = mybir.dt.float32
BF16 = mybir.dt.bfloat16
AF = mybir.ActivationFunctionType
ALU = mybir.AluOpType
AX = mybir.AxisListType

D = 1024
EPS = 1e-5

ENGS = ("sync", "gpsimd", "tensor", "vector", "scalar")


class _Slot:
    __slots__ = ("h", "count")
    def __init__(self, h):
        self.h = h
        self.count = 0


class _Node:
    __slots__ = ("nid", "eng", "fn", "deps", "slot", "target", "milestone")


class G:

    def __init__(self, nc):
        self.nc = nc
        self.nodes = []
        self.eng_sem = {e: nc.alloc_semaphore(name=f"es_{e}") for e in ENGS}
        self.last_w = {}
        self.readers = {}
        self.gdep = {e: None for e in ENGS}
        self._slots = {}
        self._slot_last = {}

    def slot(self, name):
        s = self._slots.get(name)
        if s is None:
            s = _Slot(self.nc.alloc_semaphore(name=f"sl_{name}"))
            self._slots[name] = s
        return s

    def node(self, eng, fn, reads=(), writes=(), deps=(), slot=None):
        nid = len(self.nodes)
        d = set(deps)
        for b in reads:
            w = self.last_w.get(b)
            if w is not None:
                d.add(w)
        for b in writes:
            w = self.last_w.get(b)
            if w is not None:
                d.add(w)
            d.update(self.readers.get(b, ()))
        if self.gdep[eng] is not None:
            d.add(self.gdep[eng])
        d.discard(nid)
        n = _Node()
        n.nid = nid
        n.eng = eng
        n.fn = fn
        n.slot = slot
        n.target = None
        if slot is not None:
            slot.count += 16
            n.target = slot.count
            self._slot_last[id(slot)] = nid
        n.deps = sorted(d)
        self.nodes.append(n)
        for b in reads:
            self.readers.setdefault(b, set()).add(nid)
        for b in writes:
            self.last_w[b] = nid
            self.readers[b] = set()
        return nid

    def dma(self, eng, out_ap, in_ap, reads=(), writes=(), slot=None, deps=()):
        def fn(e, o=out_ap, i=in_ap):
            return e.dma_start(out=o, in_=i)
        return self.node(eng, fn, reads, writes, deps, slot=slot)

    def barrier(self):
        tails = set()
        for w in self.last_w.values():
            tails.add(w)
        for rs in self.readers.values():
            tails.update(rs)
        tails.update(self._slot_last.values())
        ids = {}
        for e in ENGS:
            ids[e] = self.node(e, lambda en: en.nop(), deps=tuple(tails))
        for e in ENGS:
            self.gdep[e] = ids[e]

    def emit(self):
        nodes = self.nodes
        needed = [False] * len(nodes)
        for n in nodes:
            for dd in n.deps:
                needed[dd] = True
        cnt = {e: 0 for e in ENGS}
        for n in nodes:
            if n.slot is not None:
                n.milestone = (n.slot.h, n.target)
            elif needed[n.nid]:
                cnt[n.eng] += 1
                n.milestone = (self.eng_sem[n.eng], cnt[n.eng])
            else:
                n.milestone = None
        per = {e: [n for n in nodes if n.eng == e] for e in ENGS}
        nc = self.nc
        with nc.Block() as block:
            def mkbody(ename):
                def body(eng):
                    wm = {}
                    for n in per[ename]:
                        for dd in n.deps:
                            h, v = nodes[dd].milestone
                            k = id(h)
                            if wm.get(k, 0) < v:
                                eng.wait_ge(h, v)
                                wm[k] = v
                        instr = n.fn(eng)
                        if n.slot is not None:
                            instr.then_inc(n.slot.h, 16)
                        elif n.milestone is not None:
                            instr.then_inc(n.milestone[0], 1)
                return body
            for e in ENGS:
                getattr(block, e)(mkbody(e))


def build_program(T=4096, debug=()):
    import contextlib
    nc = bass.Bass("TRN2", target_bir_lowering=False)
    g = G(nc)

    S1 = 512
    G1 = S1 // 128
    NT1 = T // S1
    QB = 512
    NQB = T // QB
    S3 = 256
    G3 = S3 // 128
    NT3 = T // S3

    def dbg(name):
        return "ExternalOutput" if name in debug else "Internal"

    x_in = nc.dram_tensor("x", [T, D], F32, kind="ExternalInput").ap()
    out_hbm = nc.dram_tensor("out", [T, D], F32, kind="ExternalOutput").ap()
    wd = {}
    for nm in ("a_wq", "a_wk", "a_wv", "a_wo", "f_wr", "f_wk", "f_wv"):
        wd[nm] = nc.dram_tensor(nm, [D, D], F32, kind="ExternalInput").ap()
    mur_d = nc.dram_tensor("mu_r", [D], F32, kind="ExternalInput").ap()
    muk_d = nc.dram_tensor("mu_k", [D], F32, kind="ExternalInput").ap()

    S_x1 = nc.dram_tensor("S_x1", [T, D], F32, kind=dbg("S_x1")).ap()
    S_K = nc.dram_tensor("S_K", [D, T], BF16, kind=dbg("S_K")).ap()
    S_V = nc.dram_tensor("S_V", [T, D], BF16, kind=dbg("S_V")).ap()
    S_q = nc.dram_tensor("S_q", [D, T], BF16, kind=dbg("S_q")).ap()
    S_ctx = nc.dram_tensor("S_ctx", [T, D], BF16, kind=dbg("S_ctx")).ap()
    S_x2 = nc.dram_tensor("S_x2", [T, D], F32, kind=dbg("S_x2")).ap()

    idf_d = nc.inline_tensor(np.eye(128, dtype=np.float32), "idf").ap()
    idb_d = nc.inline_tensor(np.eye(128, dtype=ml_dtypes.bfloat16), "idb").ap()

    idf = nc.alloc_sbuf_tensor("idf_sb", [128, 128], F32)
    idb = nc.alloc_sbuf_tensor("idb_sb", [128, 128], BF16)
    mur_tm = nc.alloc_sbuf_tensor("mur_tm", [8, 128], F32)
    muk_tm = nc.alloc_sbuf_tensor("muk_tm", [8, 128], F32)
    mur = nc.alloc_sbuf_tensor("mur_fm", [128, 8], F32)
    muk = nc.alloc_sbuf_tensor("muk_fm", [128, 8], F32)
    lastc = nc.alloc_sbuf_tensor("lastc", [128, 8], BF16)
    epsb = nc.alloc_sbuf_tensor("epsb", [128, 1], F32)
    bn6 = nc.alloc_sbuf_tensor("bn6", [128, 12 * 4], F32)
    mv = [nc.alloc_sbuf_tensor(f"mv{k}", [128, 2 * 4], F32) for k in range(2)]
    vtmp = nc.alloc_sbuf_tensor("vtmp", [128, 4], F32)
    vtmp2 = nc.alloc_sbuf_tensor("vtmp2", [128, 4], F32)
    rstd = [nc.alloc_sbuf_tensor(f"rstd{k}", [128, 4], F32) for k in range(2)]
    wl = [nc.alloc_sbuf_tensor(f"wl_{k}", [128, 1024], BF16) for k in range(4)]

    mm = [nc.alloc_psum_tensor(f"mm{k}", [128, 512], F32) for k in range(2)]
    pt = [nc.alloc_psum_tensor(f"pt{k}", [128, 1024], BF16) for k in range(2)]
    big = [nc.alloc_psum_tensor(f"big{k}", [128, 1024], F32) for k in range(2)]

    mv3 = [m.ap().rearrange("p (gr two) -> p gr two", two=2) for m in mv]

    g.dma("sync", idf[:], idf_d, writes=("IDF",), slot=g.slot("idf"))
    g.dma("sync", idb[:], idb_d, writes=("IDB",), slot=g.slot("idb"))
    g.dma("sync", mur_tm[:], mur_d.rearrange("(c p) -> c p", p=128),
          writes=("MURT",), slot=g.slot("murt"))
    g.dma("sync", muk_tm[:], muk_d.rearrange("(c p) -> c p", p=128),
          writes=("MUKT",), slot=g.slot("mukt"))
    g.node("tensor", lambda e: e.transpose(big[0][:, 0:8], mur_tm[:], idf[0:8, 0:8]),
           reads=("MURT", "IDF"), writes=("BIG0",))
    g.node("vector", lambda e: e.tensor_copy(mur[:], big[0][:, 0:8]),
           reads=("BIG0",), writes=("MUR",))
    g.node("tensor", lambda e: e.transpose(big[1][:, 0:8], muk_tm[:], idf[0:8, 0:8]),
           reads=("MUKT", "IDF"), writes=("BIG1",))
    g.node("vector", lambda e: e.tensor_copy(muk[:], big[1][:, 0:8]),
           reads=("BIG1",), writes=("MUK",))
    g.node("gpsimd", lambda e: e.memset(epsb[:], EPS), writes=("EPSB",))

    def prep_loads(nm, count=4):
        w3 = wd[nm].rearrange("(a p) i -> a p i", p=128)
        for oc in range(count):
            g.dma("gpsimd", wl[oc % 4][:], w3[oc], writes=(f"WL{oc % 4}",),
                  slot=g.slot(f"WL{oc % 4}"))

    def prep_weight(nm, wT, wT_key, skip_loads=0):
        w3 = wd[nm].rearrange("(a p) i -> a p i", p=128)
        wT3 = wT.ap().rearrange("p (c o) -> p c o", o=1024)
        for oc in range(8):
            wli = wl[oc % 4]
            if oc >= skip_loads:
                g.dma("gpsimd", wli[:], w3[oc], writes=(f"WL{oc % 4}",),
                      slot=g.slot(f"WL{oc % 4}"))
            ptk = oc % 2
            def trf(e, wli=wli, ptk=ptk):
                last = None
                for ic in range(8):
                    last = e.transpose(pt[ptk][:, 128 * ic:128 * ic + 128],
                                       wli[:, 128 * ic:128 * ic + 128], idb[:])
                return last
            g.node("tensor", trf, reads=(f"WL{oc % 4}", "IDB"),
                   writes=(f"PT{ptk}",))
            def evf(e, ptk=ptk, oc=oc, wT3=wT3):
                return e.copy(wT3[:, :, 128 * oc:128 * oc + 128],
                              pt[ptk].ap().rearrange("p (c q) -> p c q", q=128))
            g.node("scalar", evf, reads=(f"PT{ptk}",), writes=(wT_key,))

    def ln_pre(xbuf_key, x3, ngroups, sb):
        def bnf(e, x3=x3, ngroups=ngroups):
            last = None
            for gq in range(ngroups):
                for h in range(2):
                    last = e.bn_stats(bn6[:, 12 * gq + 6 * h:12 * gq + 6 * h + 6],
                                      x3[:, gq, 512 * h:512 * h + 512])
            return last
        g.node("vector", bnf, reads=(xbuf_key,), writes=("BN6",))
        def agf(e, ngroups=ngroups, sb=sb):
            last = None
            for gq in range(ngroups):
                last = e.bn_aggr(mv[sb][:, 2 * gq:2 * gq + 2],
                                 bn6[:, 12 * gq:12 * gq + 12])
            return last
        g.node("vector", agf, reads=("BN6",), writes=(f"MV{sb}",))
        g.node("vector", lambda e, n=ngroups, sb=sb: e.tensor_scalar_add(
            vtmp2[:, 0:n], mv3[sb][:, 0:n, 1], EPS),
            reads=(f"MV{sb}",), writes=("VT2",))
        g.node("scalar", lambda e, n=ngroups: e.activation(
            vtmp[:, 0:n], vtmp2[:, 0:n], AF.Ln),
            reads=("VT2",), writes=("VTMP",))
        g.node("scalar", lambda e, n=ngroups, sb=sb: e.activation(
            rstd[sb][:, 0:n], vtmp[:, 0:n], AF.Exp, scale=-0.5),
            reads=("VTMP",), writes=(f"RSTD{sb}",))

    def ln_post(xbuf_key, x3, xn3, ngroups, xn_key, sb):
        def nrm(e, x3=x3, xn3=xn3, ngroups=ngroups, sb=sb):
            last = None
            for gq in range(ngroups):
                last = e.tensor_scalar(xn3[:, gq, :], x3[:, gq, :],
                                       mv3[sb][:, gq, 0:1], rstd[sb][:, gq:gq + 1],
                                       ALU.subtract, ALU.mult)
            return last
        g.node("vector", nrm, reads=(xbuf_key, f"MV{sb}", f"RSTD{sb}"),
               writes=(xn_key,))

    def tm_to_fm(src3, src_key, dst3, dst_key, ngroups, gcol, cnt):
        for gq in range(ngroups):
            a = cnt[0] % 2
            cnt[0] += 1
            ps, pkey = pt[a], f"PT{a}"
            def trf(e, src3=src3, gq=gq, ps=ps):
                last = None
                for c in range(8):
                    last = e.transpose(ps[:, 128 * c:128 * c + 128],
                                       src3[:, gq, 128 * c:128 * c + 128], idb[:])
                return last
            g.node("tensor", trf, reads=(src_key, "IDB"), writes=(pkey,))
            def evf(e, ps=ps, dst3=dst3, gq=gq):
                return e.copy(dst3[:, :, gcol(gq):gcol(gq) + 128],
                              ps.ap().rearrange("p (c q) -> p c q", q=128))
            g.node("scalar", evf, reads=(pkey,), writes=(dst_key,))

    def ffn_mix(S, xnf3, xnf_key, t1, t1_key, xr, xr_key, xk, xk_key):
        g.node("vector", lambda e: e.tensor_sub(
            t1.ap().rearrange("p (c s) -> p c s", s=S),
            xnf3[:, :, 1:S + 1], xnf3[:, :, 0:S]),
            reads=(xnf_key,), writes=(t1_key,))
        def mixf(e, mu, dst, S=S):
            last = None
            t13 = t1.ap().rearrange("p (c s) -> p c s", s=S)
            d3 = dst.ap().rearrange("p (c s) -> p c s", s=S)
            for c in range(8):
                last = e.scalar_tensor_tensor(d3[:, c, :], t13[:, c, :],
                                              mu[:, c:c + 1], xnf3[:, c, 0:S],
                                              ALU.mult, ALU.add)
            return last
        g.node("vector", lambda e: mixf(e, mur, xr), reads=(t1_key, xnf_key, "MUR"),
               writes=(xr_key,))
        g.node("vector", lambda e: mixf(e, muk, xk), reads=(t1_key, xnf_key, "MUK"),
               writes=(xk_key,))

    def proj_node(S, rhs, rhs_keys, wT, wT_key, oc, mmk):
        def pf(e, rhs=rhs, wT=wT, oc=oc, mmk=mmk, S=S):
            last = None
            for ic in range(8):
                last = e.matmul(mm[mmk][:, 0:S],
                                wT[:, 1024 * ic + 128 * oc:1024 * ic + 128 * oc + 128],
                                rhs[:, S * ic:S * ic + S],
                                start=(ic == 0), stop=(ic == 7))
            return last
        g.node("tensor", pf, reads=tuple(rhs_keys) + (wT_key,), writes=(f"MM{mmk}",))

    def ffn_mm(S, xr, xr_key, xk, xk_key, relu_b, relu_pfx, rbuf, r_key,
               x3, xbk, wrT, wrT_k, wkT, wkT_k, wvT, wvT_k, cnt, mid=None):
        M = S // 128
        for m in range(M):
            sl = m % 3
            if sl < 2:
                def rf(e, m=m, sl=sl, xr=xr, wrT=wrT, S=S):
                    last = None
                    for h in range(2):
                        for ic in range(8):
                            last = e.matmul(
                                big[sl][:, 512 * h:512 * h + 512],
                                xr[:, S * ic + 128 * m:S * ic + 128 * m + 128],
                                wrT[:, 1024 * ic + 512 * h:1024 * ic + 512 * h + 512],
                                start=(ic == 0), stop=(ic == 7))
                    return last
                g.node("tensor", rf, reads=(xr_key, wrT_k), writes=(f"BIG{sl}",))
                g.node("scalar", lambda e, m=m, sl=sl: e.activation(
                    rbuf[:, 1024 * m:1024 * m + 1024], big[sl][:], AF.Sigmoid),
                    reads=(f"BIG{sl}",), writes=(r_key,))
            else:
                def rf2(e, m=m, xr=xr, wrT=wrT, S=S):
                    last = None
                    for h in range(2):
                        for ic in range(8):
                            last = e.matmul(
                                mm[h][:, 0:512],
                                xr[:, S * ic + 128 * m:S * ic + 128 * m + 128],
                                wrT[:, 1024 * ic + 512 * h:1024 * ic + 512 * h + 512],
                                start=(ic == 0), stop=(ic == 7))
                    return last
                g.node("tensor", rf2, reads=(xr_key, wrT_k),
                       writes=("MM0", "MM1"))
                def sg2(e, m=m):
                    e.activation(rbuf[:, 1024 * m:1024 * m + 512],
                                 mm[0][:, 0:512], AF.Sigmoid)
                    return e.activation(rbuf[:, 1024 * m + 512:1024 * m + 1024],
                                        mm[1][:, 0:512], AF.Sigmoid)
                g.node("scalar", sg2, reads=("MM0", "MM1"), writes=(r_key,))
        if mid is not None:
            mid()
        rl_keys = tuple(f"{relu_pfx}{oc}" for oc in range(8))
        for oc in range(8):
            k = cnt[0] % 2
            cnt[0] += 1
            proj_node(S, xk, (xk_key,), wkT, wkT_k, oc, k)
            g.node("scalar", lambda e, k=k, oc=oc: e.activation(
                relu_b[:, S * oc:S * oc + S], mm[k][:, 0:S], AF.Relu),
                reads=(f"MM{k}",), writes=(rl_keys[oc],))
            g.node("gpsimd", lambda e, oc=oc: e.tensor_mul(
                relu_b[:, S * oc:S * oc + S], relu_b[:, S * oc:S * oc + S],
                relu_b[:, S * oc:S * oc + S]),
                reads=(rl_keys[oc],), writes=(rl_keys[oc],))
        for m in range(M):
            sl = m % 3
            if sl < 2:
                def vf(e, m=m, sl=sl, relu_b=relu_b, wvT=wvT, S=S):
                    last = None
                    for h in range(2):
                        for ic in range(8):
                            last = e.matmul(
                                big[sl][:, 512 * h:512 * h + 512],
                                relu_b[:, S * ic + 128 * m:S * ic + 128 * m + 128],
                                wvT[:, 1024 * ic + 512 * h:1024 * ic + 512 * h + 512],
                                start=(ic == 0), stop=(ic == 7))
                    return last
                g.node("tensor", vf, reads=rl_keys + (wvT_k,),
                       writes=(f"BIG{sl}",))
                g.node("vector", lambda e, m=m, sl=sl: e.tensor_mul(
                    big[sl][:], big[sl][:], rbuf[:, 1024 * m:1024 * m + 1024]),
                    reads=(f"BIG{sl}", r_key), writes=(f"BIG{sl}",))
                g.node("vector", lambda e, m=m, sl=sl, x3=x3: e.scalar_tensor_tensor(
                    x3[:, m, :], big[sl][:], 0.5, x3[:, m, :], ALU.mult, ALU.add),
                    reads=(f"BIG{sl}", xbk), writes=(xbk,))
            else:
                def vf2(e, m=m, relu_b=relu_b, wvT=wvT, S=S):
                    last = None
                    for h in range(2):
                        for ic in range(8):
                            last = e.matmul(
                                mm[h][:, 0:512],
                                relu_b[:, S * ic + 128 * m:S * ic + 128 * m + 128],
                                wvT[:, 1024 * ic + 512 * h:1024 * ic + 512 * h + 512],
                                start=(ic == 0), stop=(ic == 7))
                    return last
                g.node("tensor", vf2, reads=rl_keys + (wvT_k,),
                       writes=("MM0", "MM1"))
                def gt2(e, m=m):
                    e.tensor_mul(mm[0][:, 0:512], mm[0][:, 0:512],
                                 rbuf[:, 1024 * m:1024 * m + 512])
                    return e.tensor_mul(mm[1][:, 0:512], mm[1][:, 0:512],
                                        rbuf[:, 1024 * m + 512:1024 * m + 1024])
                g.node("vector", gt2, reads=("MM0", "MM1", r_key),
                       writes=("MM0", "MM1"))
                def st2(e, m=m, x3=x3):
                    e.scalar_tensor_tensor(x3[:, m, 0:512], mm[0][:, 0:512],
                                           0.5, x3[:, m, 0:512],
                                           ALU.mult, ALU.add)
                    return e.scalar_tensor_tensor(
                        x3[:, m, 512:1024], mm[1][:, 0:512], 0.5,
                        x3[:, m, 512:1024], ALU.mult, ALU.add)
                g.node("vector", st2, reads=("MM0", "MM1", xbk),
                       writes=(xbk,))

    st = contextlib.ExitStack()
    wrT = st.enter_context(nc.sbuf_tensor("wrT", [128, 8 * 1024], BF16))
    wkT = st.enter_context(nc.sbuf_tensor("wkT", [128, 8 * 1024], BF16))
    wvT = st.enter_context(nc.sbuf_tensor("wvT", [128, 8 * 1024], BF16))
    x01 = [st.enter_context(nc.sbuf_tensor(f"x01_{k}", [128, G1 * 1024], F32))
           for k in range(4)]
    xn_tm = st.enter_context(nc.sbuf_tensor("xn_tm", [128, G1 * 1024], BF16))
    xnf = st.enter_context(nc.sbuf_tensor("xnf", [128, 8 * (S1 + 1)], BF16))
    t1b = st.enter_context(nc.sbuf_tensor("t1b", [128, 8 * S1], BF16))
    xrb = [st.enter_context(nc.sbuf_tensor(f"xrb_{k}", [128, 8 * S1], BF16))
           for k in range(2)]
    xkb = [st.enter_context(nc.sbuf_tensor(f"xkb_{k}", [128, 8 * S1], BF16))
           for k in range(2)]
    relu1 = st.enter_context(nc.sbuf_tensor("relu1", [128, 8 * S1], BF16))
    rb = st.enter_context(nc.sbuf_tensor("rb", [128, 8 * S1], F32))

    x_t = x_in.rearrange("(n g p) d -> n p g d", g=G1, p=128)
    sx1_t = S_x1.rearrange("(n g p) d -> n p g d", g=G1, p=128)
    xnf3 = xnf.ap().rearrange("p (c s) -> p c s", s=S1 + 1)
    cnt1 = [0]

    def p1a_load(i):
        xbk = f"X01_{i % 4}"
        x3 = x01[i % 4].ap().rearrange("p (gr d) -> p gr d", d=1024)
        g.dma("sync", x3, x_t[i], writes=(xbk,), slot=g.slot(xbk))

    def p1a_stats(i):
        xbk = f"X01_{i % 4}"
        x3 = x01[i % 4].ap().rearrange("p (gr d) -> p gr d", d=1024)
        ln_pre(xbk, x3, G1, i % 2)

    def p1a_nrm(i):
        xbk = f"X01_{i % 4}"
        x3 = x01[i % 4].ap().rearrange("p (gr d) -> p gr d", d=1024)
        xn3 = xn_tm.ap().rearrange("p (gr d) -> p gr d", d=1024)
        ln_post(xbk, x3, xn3, G1, "XN", i % 2)

    def p1a_fm(i):
        xn3 = xn_tm.ap().rearrange("p (gr d) -> p gr d", d=1024)
        if i == 0:
            g.node("gpsimd", lambda e: e.memset(xnf3[:, :, 0], 0.0),
                   writes=("XNF",))
        else:
            g.node("gpsimd", lambda e: e.tensor_copy(xnf3[:, :, 0], lastc[:, 0:8]),
                   reads=("LASTC",), writes=("XNF",))
        tm_to_fm(xn3, "XN", xnf3, "XNF", G1, lambda gq: 1 + 128 * gq, cnt1)
        g.node("gpsimd", lambda e: e.tensor_copy(lastc[:, 0:8], xnf3[:, :, S1]),
               reads=("XNF",), writes=("LASTC",))
        ffn_mix(S1, xnf3, "XNF", t1b, "T1", xrb[i % 2], f"XR{i % 2}",
                xkb[i % 2], f"XK{i % 2}")

    def p1a_B1(i):
        def mid(i=i):
            if i + 2 < NT1:
                p1a_stats(i + 2)
            if i + 1 < NT1:
                p1a_fm(i + 1)
        xbk = f"X01_{i % 4}"
        x3 = x01[i % 4].ap().rearrange("p (gr d) -> p gr d", d=1024)
        ffn_mm(S1, xrb[i % 2], f"XR{i % 2}", xkb[i % 2], f"XK{i % 2}",
               relu1, "RL1_", rb, "RB", x3, xbk,
               wrT, "WRT", wkT, "WKT", wvT, "WVT", cnt1, mid=mid)

    def p1a_B2(i):
        xbk = f"X01_{i % 4}"
        x3 = x01[i % 4].ap().rearrange("p (gr d) -> p gr d", d=1024)
        g.dma("gpsimd", sx1_t[i], x3, reads=(xbk,), writes=("RG_X1",),
              slot=g.slot(f"x1st_{i % 4}"))

    prep_weight("f_wr", wrT, "WRT")
    p1a_load(0)
    if NT1 > 1:
        p1a_load(1)
    p1a_stats(0)
    p1a_nrm(0)
    p1a_fm(0)
    prep_weight("f_wk", wkT, "WKT")
    if NT1 > 2:
        p1a_load(2)
    if NT1 > 1:
        p1a_stats(1)
        p1a_nrm(1)
    prep_weight("f_wv", wvT, "WVT")
    if NT1 > 3:
        p1a_load(3)
    for i in range(NT1):
        p1a_B1(i)
        p1a_B2(i)
        if i + 4 < NT1:
            p1a_load(i + 4)
        if i + 2 < NT1:
            p1a_nrm(i + 2)

    prep_loads("a_wq")
    g.barrier()
    st.close()

    st = contextlib.ExitStack()
    wqT = st.enter_context(nc.sbuf_tensor("wqT", [128, 8 * 1024], BF16))
    wkTa = st.enter_context(nc.sbuf_tensor("wkTa", [128, 8 * 1024], BF16))
    wvTa = st.enter_context(nc.sbuf_tensor("wvTa", [128, 8 * 1024], BF16))
    x1t = [st.enter_context(nc.sbuf_tensor(f"x1t_{k}", [128, G1 * 1024], F32))
           for k in range(4)]
    xn2b = st.enter_context(nc.sbuf_tensor("xn2b", [128, G1 * 1024], BF16))
    xn2f = [st.enter_context(nc.sbuf_tensor(f"xn2f_{k}", [128, 8 * S1], BF16))
            for k in range(2)]
    qb = st.enter_context(nc.sbuf_tensor("qb", [128, 8 * S1], BF16))
    kbb = st.enter_context(nc.sbuf_tensor("kbb", [128, 8 * S1], BF16))
    vbb = st.enter_context(nc.sbuf_tensor("vbb", [128, G1 * 1024], BF16))

    sq_t = S_q.rearrange("(c p) t -> p c t", p=128)
    sk_t = S_K.rearrange("(c p) t -> p c t", p=128)
    sv_t = S_V.rearrange("(n g p) d -> n p g d", g=G1, p=128)
    cnt2 = [0]

    def p1b_load(i):
        xbk = f"X1T_{i % 4}"
        x3 = x1t[i % 4].ap().rearrange("p (gr d) -> p gr d", d=1024)
        g.dma("sync", x3, sx1_t[i], reads=("RG_X1",), writes=(xbk,),
              slot=g.slot(xbk))

    def p1b_stats(i):
        xbk = f"X1T_{i % 4}"
        x3 = x1t[i % 4].ap().rearrange("p (gr d) -> p gr d", d=1024)
        ln_pre(xbk, x3, G1, i % 2)

    def p1b_post(i):
        xb = x1t[i % 4]
        xbk = f"X1T_{i % 4}"
        x3 = xb.ap().rearrange("p (gr d) -> p gr d", d=1024)
        xn3 = xn2b.ap().rearrange("p (gr d) -> p gr d", d=1024)
        ln_post(xbk, x3, xn3, G1, "XN2B", i % 2)
        xn2f3 = xn2f[i % 2].ap().rearrange("p (c s) -> p c s", s=S1)
        tm_to_fm(xn3, "XN2B", xn2f3, f"XN2F{i % 2}", G1,
                 lambda gq: 128 * gq, cnt2)

    def p1b_B(i, mid=None):
        xf = xn2f[i % 2]
        xfk = f"XN2F{i % 2}"
        for dst, dkey, wT, wkey in ((qb, "QB", wqT, "AWQ"), (kbb, "KB", wkTa, "AWK")):
            for oc in range(8):
                k = cnt2[0] % 2
                cnt2[0] += 1
                proj_node(S1, xf, (xfk,), wT, wkey, oc, k)
                g.node("scalar", lambda e, k=k, oc=oc, dst=dst: e.copy(
                    dst[:, S1 * oc:S1 * oc + S1], mm[k][:, 0:S1]),
                    reads=(f"MM{k}",), writes=(dkey,))
        if mid is not None:
            mid()
        vb3 = vbb.ap().rearrange("p (gr d) -> p gr d", d=1024)
        for gq in range(G1):
            sl = gq % 3
            if sl < 2:
                def pvf(e, gq=gq, sl=sl, xf=xf):
                    last = None
                    for h in range(2):
                        for ic in range(8):
                            last = e.matmul(
                                big[sl][:, 512 * h:512 * h + 512],
                                xf[:, S1 * ic + 128 * gq:S1 * ic + 128 * gq + 128],
                                wvTa[:, 1024 * ic + 512 * h:1024 * ic + 512 * h + 512],
                                start=(ic == 0), stop=(ic == 7))
                    return last
                g.node("tensor", pvf, reads=(xfk, "AWV"), writes=(f"BIG{sl}",))
                g.node("scalar", lambda e, gq=gq, sl=sl, vb3=vb3: e.copy(
                    vb3[:, gq, :], big[sl][:]),
                    reads=(f"BIG{sl}",), writes=("VB",))
            else:
                def pvf2(e, gq=gq, xf=xf):
                    last = None
                    for h in range(2):
                        for ic in range(8):
                            last = e.matmul(
                                mm[h][:, 0:512],
                                xf[:, S1 * ic + 128 * gq:S1 * ic + 128 * gq + 128],
                                wvTa[:, 1024 * ic + 512 * h:1024 * ic + 512 * h + 512],
                                start=(ic == 0), stop=(ic == 7))
                    return last
                g.node("tensor", pvf2, reads=(xfk, "AWV"),
                       writes=("MM0", "MM1"))
                def cpv2(e, gq=gq, vb3=vb3):
                    e.copy(vb3[:, gq, 0:512], mm[0][:, 0:512])
                    return e.copy(vb3[:, gq, 512:1024], mm[1][:, 0:512])
                g.node("scalar", cpv2, reads=("MM0", "MM1"), writes=("VB",))
        qb3 = qb.ap().rearrange("p (c s) -> p c s", s=S1)
        kb3 = kbb.ap().rearrange("p (c s) -> p c s", s=S1)
        g.dma("gpsimd", sq_t[:, :, S1 * i:S1 * i + S1], qb3, reads=("QB",),
              writes=("RG_q",), slot=g.slot("qbst"))
        g.dma("gpsimd", sk_t[:, :, S1 * i:S1 * i + S1], kb3, reads=("KB",),
              writes=("RG_K",), slot=g.slot("kbst"))
        g.dma("gpsimd", sv_t[i], vb3, reads=("VB",), writes=("RG_V",),
              slot=g.slot("vbst"))

    prep_weight("a_wq", wqT, "AWQ", skip_loads=4)
    p1b_load(0)
    if NT1 > 1:
        p1b_load(1)
    p1b_stats(0)
    p1b_post(0)
    prep_weight("a_wk", wkTa, "AWK")
    if NT1 > 2:
        p1b_load(2)
    if NT1 > 1:
        p1b_stats(1)
    prep_weight("a_wv", wvTa, "AWV")
    for i in range(NT1):
        mid = (lambda i2=i + 2: p1b_stats(i2)) if i + 2 < NT1 else None
        p1b_B(i, mid=mid)
        if i + 3 < NT1:
            p1b_load(i + 3)
        if i + 1 < NT1:
            p1b_post(i + 1)

    g.barrier()
    st.close()

    st = contextlib.ExitStack()
    qblk = [st.enter_context(nc.sbuf_tensor(f"qblk_{k}", [128, 8 * 512], BF16))
            for k in range(2)]
    Kb = [st.enter_context(nc.sbuf_tensor(f"Kb_{k}", [128, 8 * 512], BF16))
          for k in range(3)]
    Vb = [st.enter_context(nc.sbuf_tensor(f"Vb_{k}", [128, 4 * 512], BF16))
          for k in range(4)]
    srow = [st.enter_context(nc.sbuf_tensor(f"srow_{s}", [128, T], F32))
            for s in range(4)]
    prow = [st.enter_context(nc.sbuf_tensor(f"prow_{s}", [128, T], BF16))
            for s in range(4)]
    pTs4 = [[st.enter_context(nc.sbuf_tensor(f"pTs_{s}_{j}", [128, 512], BF16))
             for j in range(NQB)] for s in range(4)]
    ctxsb = st.enter_context(nc.sbuf_tensor("ctxsb", [128, 4 * 1024], BF16))
    masks = st.enter_context(nc.sbuf_tensor("masks", [128, 896], BF16))
    nmx = [st.enter_context(nc.sbuf_tensor(f"nmx_{k}", [128, 4], F32))
           for k in range(2)]
    cmx = [st.enter_context(nc.sbuf_tensor(f"cmx_{k}", [128, 4 * 8], F32))
           for k in range(2)]
    lsm = [st.enter_context(nc.sbuf_tensor(f"lsm_{k}", [128, 4], F32))
           for k in range(2)]
    lsmh = [st.enter_context(nc.sbuf_tensor(f"lsmh_{k}", [128, 4], F32))
            for k in range(2)]
    liv = [st.enter_context(nc.sbuf_tensor(f"liv_{k}", [128, 4], F32))
           for k in range(2)]
    sctx_t = S_ctx.rearrange("(n g p) d -> n p g d", g=4, p=128)

    g.node("gpsimd", lambda e: e.memset(masks[:], 0.0), writes=("MASKS",))
    def mfn(e):
        return e.affine_select(masks[:], masks[:], [[-1, 896]], ALU.is_ge,
                               -1e30, base=384, channel_multiplier=1)
    g.node("gpsimd", mfn, reads=("MASKS",), writes=("MASKS",))

    pcnt = [0]

    def p2_A(b):
        par = b % 2
        qk = f"QBLK_{par}"
        g.dma("sync", qblk[par].ap().rearrange("p (c s) -> p c s", s=512),
              sq_t[:, :, QB * b:QB * b + QB], reads=("RG_q",), writes=(qk,),
              slot=g.slot(qk))
        for kbi in range(b + 1):
            kkey = f"KB_{kbi % 3}"
            g.dma("sync", Kb[kbi % 3].ap().rearrange("p (c s) -> p c s", s=512),
                  sk_t[:, :, 512 * kbi:512 * kbi + 512], reads=("RG_K",),
                  writes=(kkey,), slot=g.slot(kkey))
            for s in range(4):
                k = pcnt[0] % 2
                pcnt[0] += 1
                N = 128 * (s + 1) if kbi == b else 512
                def sf(e, k=k, s=s, kbi=kbi, par=par, N=N):
                    last = None
                    qh = qblk[par]
                    for ic in range(8):
                        last = e.matmul(
                            mm[k][:, 0:N],
                            qh[:, 512 * ic + 128 * s:512 * ic + 128 * s + 128],
                            Kb[kbi % 3][:, 512 * ic:512 * ic + N],
                            start=(ic == 0), stop=(ic == 7))
                    return last
                g.node("tensor", sf, reads=(qk, kkey), writes=(f"MM{k}",))
                if s % 2 == 0 and kbi >= 3:
                    g.node("scalar", lambda e, k=k, s=s, kbi=kbi, N=N: e.activation(
                        srow[s][:, 512 * kbi:512 * kbi + N], mm[k][:, 0:N],
                        AF.Copy, scale=0.03125),
                        reads=(f"MM{k}",), writes=(f"SROW{s}",))
                else:
                    g.node("vector", lambda e, k=k, s=s, kbi=kbi, N=N: e.tensor_scalar_mul(
                        srow[s][:, 512 * kbi:512 * kbi + N], mm[k][:, 0:N],
                        0.03125),
                        reads=(f"MM{k}",), writes=(f"SROW{s}",))
                if kbi == b:
                    off = 384 - 128 * s
                    g.node("vector", lambda e, s=s, b=b, off=off, N=N: e.tensor_add(
                        srow[s][:, 512 * b:512 * b + N],
                        srow[s][:, 512 * b:512 * b + N],
                        masks[:, off:off + N]),
                        reads=(f"SROW{s}", "MASKS"), writes=(f"SROW{s}",))
                g.node("vector", lambda e, s=s, kbi=kbi, par=par, N=N: e.reduce_max(
                    cmx[par][:, 8 * s + kbi:8 * s + kbi + 1],
                    srow[s][:, 512 * kbi:512 * kbi + N], AX.X),
                    reads=(f"SROW{s}",), writes=(f"CMX{par}",))
        n = 512 * (b + 1)
        for s in range(4):
            g.node("vector", lambda e, s=s, b=b, par=par: e.reduce_max(
                nmx[par][:, s:s + 1], cmx[par][:, 8 * s:8 * s + b + 1], AX.X,
                negate=True),
                reads=(f"CMX{par}",), writes=(f"NMX{par}",))

    def p2_Aexp(b):
        par = b % 2
        n = 512 * (b + 1)
        nh = (n // 1024) * 512 if n >= 1024 else n
        for s in range(4):
            ne = min(nh, 512 * b + 128 * (s + 1))
            g.node("scalar", lambda e, s=s, ne=ne, par=par: e.activation(
                prow[s][:, 0:ne], srow[s][:, 0:ne], AF.Exp,
                bias=nmx[par][:, s:s + 1], accum_out=lsmh[0][:, s:s + 1]),
                reads=(f"SROW{s}", f"NMX{par}"),
                writes=(f"PROW_{s}_0", "LSMH0"))
        if nh < n:
            for s in range(4):
                ns_ = 512 * b + 128 * (s + 1)
                g.node("scalar", lambda e, s=s, nh=nh, ns_=ns_, par=par: e.activation(
                    prow[s][:, nh:ns_], srow[s][:, nh:ns_], AF.Exp,
                    bias=nmx[par][:, s:s + 1], accum_out=lsmh[1][:, s:s + 1]),
                    reads=(f"SROW{s}", f"NMX{par}"),
                    writes=(f"PROW_{s}_1", "LSMH1"))
            g.node("vector", lambda e, par=par: e.tensor_add(
                lsm[par][:, 0:4], lsmh[0][:, 0:4], lsmh[1][:, 0:4]),
                reads=("LSMH0", "LSMH1"), writes=(f"LSM{par}",))
        else:
            g.node("vector", lambda e, par=par: e.tensor_copy(
                lsm[par][:, 0:4], lsmh[0][:, 0:4]),
                reads=("LSMH0",), writes=(f"LSM{par}",))
        for s in range(4):
            g.node("vector", lambda e, s=s, par=par: e.reciprocal(
                liv[par][:, s:s + 1], lsm[par][:, s:s + 1]),
                reads=(f"LSM{par}",), writes=(f"LIV{par}",))

    def p2_prefV(b):
        for kbi in range(min(4, b + 1)):
            vkey = f"VB_{kbi % 4}"
            g.dma("gpsimd",
                  Vb[kbi % 4].ap().rearrange("p (gr d) -> p gr d", d=512),
                  sv_t[kbi][:, :, 0:512], reads=("RG_V",), writes=(vkey,),
                  slot=g.slot(vkey))

    ACCK = ("B0L", "B0R", "B1L", "B1R")

    def acc_ap(s):
        return big[s // 2][:, 512 * (s % 2):512 * (s % 2) + 512]

    def p2_Bhalf(b, fh):
        par = b % 2
        n = 512 * (b + 1)
        nh = (n // 1024) * 512 if n >= 1024 else n
        for kbi in range(b + 1):
            vkey = f"VB_{kbi % 4}"
            if fh == 1 or kbi >= min(4, b + 1):
                g.dma("gpsimd",
                      Vb[kbi % 4].ap().rearrange("p (gr d) -> p gr d", d=512),
                      sv_t[kbi][:, :, 512 * fh:512 * fh + 512],
                      reads=("RG_V",), writes=(vkey,), slot=g.slot(vkey))
            half = 0 if 512 * (kbi + 1) <= nh else 1
            ptk = tuple(f"PTS{s}_{kbi}" for s in range(4))
            if fh == 0:
                for sp in range(2):
                    a = pcnt[0] % 2
                    pcnt[0] += 1
                    def trf(e, sp=sp, kbi=kbi, b=b, a=a):
                        last = None
                        for si in range(2):
                            s = 2 * sp + si
                            jmax = s if kbi == b else 3
                            for j in range(jmax + 1):
                                last = e.transpose(
                                    pt[a][:, 512 * si + 128 * j:512 * si + 128 * j + 128],
                                    prow[s][:, 512 * kbi + 128 * j:512 * kbi + 128 * j + 128],
                                    idb[:])
                        return last
                    g.node("tensor", trf,
                           reads=(f"PROW_{2 * sp}_{half}",
                                  f"PROW_{2 * sp + 1}_{half}", "IDB"),
                           writes=(f"PT{a}",))
                    def cpf(e, sp=sp, kbi=kbi, b=b, a=a):
                        last = None
                        for si in range(2):
                            s = 2 * sp + si
                            jmax = s if kbi == b else 3
                            last = e.copy(
                                pTs4[s][kbi][:, 0:128 * (jmax + 1)],
                                pt[a][:, 512 * si:512 * si + 128 * (jmax + 1)])
                        return last
                    g.node("scalar", cpf, reads=(f"PT{a}",),
                           writes=(ptk[2 * sp], ptk[2 * sp + 1]))
            def avf(e, kbi=kbi, b=b):
                last = None
                vb3 = Vb[kbi % 4].ap().rearrange("p (gr d) -> p gr d", d=512)
                for s in range(4):
                    jmax = s if kbi == b else 3
                    for j in range(jmax + 1):
                        last = e.matmul(
                            acc_ap(s),
                            pTs4[s][kbi][:, 128 * j:128 * j + 128],
                            vb3[:, j, :],
                            start=(kbi == 0 and j == 0),
                            stop=(kbi == b and j == jmax))
                return last
            g.node("tensor", avf, reads=ptk + (vkey,), writes=ACCK)
        for s in range(4):
            g.node("vector", lambda e, s=s, fh=fh, par=par: e.tensor_scalar(
                ctxsb[:, 1024 * s + 512 * fh:1024 * s + 512 * fh + 512],
                acc_ap(s), liv[par][:, s:s + 1], None, ALU.mult),
                reads=(ACCK[s], f"LIV{par}"), writes=("CTXSB",))

    def p2_Bstore(b):
        g.dma("gpsimd", sctx_t[b],
              ctxsb.ap().rearrange("p (gr d) -> p gr d", d=1024),
              reads=("CTXSB",), writes=("RG_ctx",), slot=g.slot("ctxst"))

    p2_A(0)
    p2_Aexp(0)
    for b in range(NQB):
        if b + 1 < NQB:
            p2_A(b + 1)
        p2_prefV(b)
        p2_Bhalf(b, 0)
        if b + 1 < NQB:
            p2_Aexp(b + 1)
        p2_Bhalf(b, 1)
        p2_Bstore(b)

    prep_loads("a_wo")
    g.barrier()
    st.close()

    st = contextlib.ExitStack()
    woT = st.enter_context(nc.sbuf_tensor("woT", [128, 8 * 1024], BF16))
    x1c = [st.enter_context(nc.sbuf_tensor(f"x1c_{k}", [128, G1 * 1024], F32))
           for k in range(3)]
    ctxt = [st.enter_context(nc.sbuf_tensor(f"ctxt_{k}", [128, G1 * 1024], BF16))
            for k in range(3)]
    ctxf = [st.enter_context(nc.sbuf_tensor(f"ctxf_{k}", [128, 8 * S1], BF16))
            for k in range(3)]

    sx2_t = S_x2.rearrange("(n g p) d -> n p g d", g=G1, p=128)
    sctx_t1 = S_ctx.rearrange("(n g p) d -> n p g d", g=G1, p=128)
    cnt4 = [0]

    def p3a_A(i):
        ck_ = f"CTXT_{i % 3}"
        c3 = ctxt[i % 3].ap().rearrange("p (gr d) -> p gr d", d=1024)
        g.dma("sync", c3, sctx_t1[i], reads=("RG_ctx",), writes=(ck_,),
              slot=g.slot(ck_))
        ctxf3 = ctxf[i % 3].ap().rearrange("p (c s) -> p c s", s=S1)
        tm_to_fm(c3, ck_, ctxf3, f"CTXF{i % 3}", G1, lambda gq: 128 * gq, cnt4)

    def p3a_B(i):
        xk_ = f"X1C_{i % 3}"
        x3 = x1c[i % 3].ap().rearrange("p (gr d) -> p gr d", d=1024)
        g.dma("sync", x3, sx1_t[i], reads=("RG_X1",), writes=(xk_,),
              slot=g.slot(xk_))
        cf = ctxf[i % 3]
        cfk = f"CTXF{i % 3}"
        for gq in range(G1):
            sl = gq % 3
            if sl < 2:
                def pwf(e, gq=gq, sl=sl, cf=cf):
                    last = None
                    for h in range(2):
                        for ic in range(8):
                            last = e.matmul(
                                big[sl][:, 512 * h:512 * h + 512],
                                cf[:, S1 * ic + 128 * gq:S1 * ic + 128 * gq + 128],
                                woT[:, 1024 * ic + 512 * h:1024 * ic + 512 * h + 512],
                                start=(ic == 0), stop=(ic == 7))
                    return last
                g.node("tensor", pwf, reads=(cfk, "AWO"), writes=(f"BIG{sl}",))
                g.node("vector", lambda e, gq=gq, sl=sl, x3=x3: e.tensor_add(
                    x3[:, gq, :], big[sl][:], x3[:, gq, :]),
                    reads=(f"BIG{sl}", xk_), writes=(xk_,))
            else:
                def pwf2(e, gq=gq, cf=cf):
                    last = None
                    for h in range(2):
                        for ic in range(8):
                            last = e.matmul(
                                mm[h][:, 0:512],
                                cf[:, S1 * ic + 128 * gq:S1 * ic + 128 * gq + 128],
                                woT[:, 1024 * ic + 512 * h:1024 * ic + 512 * h + 512],
                                start=(ic == 0), stop=(ic == 7))
                    return last
                g.node("tensor", pwf2, reads=(cfk, "AWO"),
                       writes=("MM0", "MM1"))
                def ad2(e, gq=gq, x3=x3):
                    e.tensor_add(x3[:, gq, 0:512], mm[0][:, 0:512],
                                 x3[:, gq, 0:512])
                    return e.tensor_add(x3[:, gq, 512:1024], mm[1][:, 0:512],
                                        x3[:, gq, 512:1024])
                g.node("vector", ad2, reads=("MM0", "MM1", xk_),
                       writes=(xk_,))
        g.dma("gpsimd", sx2_t[i], x3, reads=(xk_,), writes=("RG_X2",),
              slot=g.slot(f"x2st_{i % 3}"))

    prep_weight("a_wo", woT, "AWO", skip_loads=4)
    p3a_A(0)
    if NT1 > 1:
        p3a_A(1)
    for i in range(NT1):
        p3a_B(i)
        if i + 2 < NT1:
            p3a_A(i + 2)

    g.barrier()
    st.close()

    st = contextlib.ExitStack()
    wrT3 = st.enter_context(nc.sbuf_tensor("wrT3", [128, 8 * 1024], BF16))
    wkT3 = st.enter_context(nc.sbuf_tensor("wkT3", [128, 8 * 1024], BF16))
    wvT3 = st.enter_context(nc.sbuf_tensor("wvT3", [128, 8 * 1024], BF16))
    x2c = [st.enter_context(nc.sbuf_tensor(f"x2c_{k}", [128, G1 * 1024], F32))
           for k in range(4)]
    xn3_tm = st.enter_context(nc.sbuf_tensor("xn3_tm", [128, G1 * 1024], BF16))
    xnf_3 = st.enter_context(nc.sbuf_tensor("xnf3", [128, 8 * (S1 + 1)], BF16))
    t1c = st.enter_context(nc.sbuf_tensor("t1c", [128, 8 * S1], BF16))
    xrc = [st.enter_context(nc.sbuf_tensor(f"xrc_{k}", [128, 8 * S1], BF16))
           for k in range(2)]
    xkc = [st.enter_context(nc.sbuf_tensor(f"xkc_{k}", [128, 8 * S1], BF16))
           for k in range(2)]
    relu3 = st.enter_context(nc.sbuf_tensor("relu3", [128, 8 * S1], BF16))
    rc = st.enter_context(nc.sbuf_tensor("rc", [128, 8 * S1], F32))

    out_t1 = out_hbm.rearrange("(n g p) d -> n p g d", g=G1, p=128)
    xnf33 = xnf_3.ap().rearrange("p (c s) -> p c s", s=S1 + 1)
    cnt3 = [0]

    def p3b_load(i):
        xbk = f"X2C_{i % 4}"
        x3 = x2c[i % 4].ap().rearrange("p (gr d) -> p gr d", d=1024)
        g.dma("sync", x3, sx2_t[i], reads=("RG_X2",), writes=(xbk,),
              slot=g.slot(xbk))

    def p3b_stats(i):
        xbk = f"X2C_{i % 4}"
        x3 = x2c[i % 4].ap().rearrange("p (gr d) -> p gr d", d=1024)
        ln_pre(xbk, x3, G1, i % 2)

    def p3b_nrm(i):
        xbk = f"X2C_{i % 4}"
        x3 = x2c[i % 4].ap().rearrange("p (gr d) -> p gr d", d=1024)
        xn3 = xn3_tm.ap().rearrange("p (gr d) -> p gr d", d=1024)
        ln_post(xbk, x3, xn3, G1, "XN3B", i % 2)

    def p3b_fm(i):
        xn3 = xn3_tm.ap().rearrange("p (gr d) -> p gr d", d=1024)
        if i == 0:
            g.node("gpsimd", lambda e: e.memset(xnf33[:, :, 0], 0.0),
                   writes=("XNF3",))
        else:
            g.node("gpsimd", lambda e: e.tensor_copy(xnf33[:, :, 0], lastc[:, 0:8]),
                   reads=("LASTC",), writes=("XNF3",))
        tm_to_fm(xn3, "XN3B", xnf33, "XNF3", G1, lambda gq: 1 + 128 * gq, cnt3)
        g.node("gpsimd", lambda e: e.tensor_copy(lastc[:, 0:8], xnf33[:, :, S1]),
               reads=("XNF3",), writes=("LASTC",))
        ffn_mix(S1, xnf33, "XNF3", t1c, "T1c", xrc[i % 2], f"XRc{i % 2}",
                xkc[i % 2], f"XKc{i % 2}")

    def p3b_B1(i):
        def mid(i=i):
            if i + 2 < NT1:
                p3b_stats(i + 2)
            if i + 1 < NT1:
                p3b_fm(i + 1)
        xbk = f"X2C_{i % 4}"
        x3 = x2c[i % 4].ap().rearrange("p (gr d) -> p gr d", d=1024)
        ffn_mm(S1, xrc[i % 2], f"XRc{i % 2}", xkc[i % 2], f"XKc{i % 2}",
               relu3, "RL3_", rc, "RBc", x3, xbk,
               wrT3, "WRT3", wkT3, "WKT3", wvT3, "WVT3", cnt3, mid=mid)

    def p3b_B2(i):
        xbk = f"X2C_{i % 4}"
        x3 = x2c[i % 4].ap().rearrange("p (gr d) -> p gr d", d=1024)
        g.dma("gpsimd", out_t1[i], x3, reads=(xbk,), writes=("RG_OUT",),
              slot=g.slot(f"outst_{i % 4}"))

    prep_weight("f_wr", wrT3, "WRT3")
    p3b_load(0)
    if NT1 > 1:
        p3b_load(1)
    p3b_stats(0)
    p3b_nrm(0)
    p3b_fm(0)
    prep_weight("f_wk", wkT3, "WKT3")
    if NT1 > 2:
        p3b_load(2)
    if NT1 > 1:
        p3b_stats(1)
        p3b_nrm(1)
    prep_weight("f_wv", wvT3, "WVT3")
    if NT1 > 3:
        p3b_load(3)
    for i in range(NT1):
        p3b_B1(i)
        p3b_B2(i)
        if i + 4 < NT1:
            p3b_load(i + 4)
        if i + 2 < NT1:
            p3b_nrm(i + 2)

    g.barrier()
    st.close()
    g.emit()
    return nc


_CACHE = {}

def _get_program(T=4096, debug=()):
    key = (T, tuple(sorted(debug)))
    if key not in _CACHE:
        _CACHE[key] = build_program(T, debug)
    return _CACHE[key]


def kernel(L0_attn_in_features, L0_attn_w_q, L0_attn_w_k, L0_attn_w_v,
           L0_attn_w_o, L0_ffn_W_r, L0_ffn_W_k, L0_ffn_W_v, L0_ffn_mu_r,
           L0_ffn_mu_k):
    x = np.ascontiguousarray(np.asarray(L0_attn_in_features, dtype=np.float32))
    B, T, d = x.shape
    nc = _get_program(T)
    common = {
        "a_wq": np.ascontiguousarray(np.asarray(L0_attn_w_q, np.float32)),
        "a_wk": np.ascontiguousarray(np.asarray(L0_attn_w_k, np.float32)),
        "a_wv": np.ascontiguousarray(np.asarray(L0_attn_w_v, np.float32)),
        "a_wo": np.ascontiguousarray(np.asarray(L0_attn_w_o, np.float32)),
        "f_wr": np.ascontiguousarray(np.asarray(L0_ffn_W_r, np.float32)),
        "f_wk": np.ascontiguousarray(np.asarray(L0_ffn_W_k, np.float32)),
        "f_wv": np.ascontiguousarray(np.asarray(L0_ffn_W_v, np.float32)),
        "mu_r": np.ascontiguousarray(np.asarray(L0_ffn_mu_r, np.float32)),
        "mu_k": np.ascontiguousarray(np.asarray(L0_ffn_mu_k, np.float32)),
    }
    in_maps = [dict(common, x=np.ascontiguousarray(x[c])) for c in range(B)]
    res = run_bass_kernel_spmd(nc, in_maps, core_ids=list(range(B)))
    return np.stack([np.asarray(res.results[c]["out"], np.float32)
                     for c in range(B)], axis=0)
